# version 07
# baseline (speedup 1.0000x reference)
import numpy as np
import ml_dtypes
import concourse.bass as bass
import concourse.mybir as mybir
from concourse.bass_utils import run_bass_kernel_spmd
from contextlib import ExitStack

F32 = mybir.dt.float32
BF16 = mybir.dt.bfloat16
I8 = mybir.dt.int8
AF = mybir.ActivationFunctionType
ALU = mybir.AluOpType
AX = mybir.AxisListType

D = 4096
S = 4096
OWN = 2048
NT = D // 128
HEADS = 8
KV = 1024
SC = 0.08838834764831845
EPS = 1e-5


def build_nc():
    nc = bass.Bass("TRN2", target_bir_lowering=False)

    x = nc.dram_tensor("x", [S, D], F32, kind="ExternalInput").ap()
    wq = nc.dram_tensor("wq", [D, D], F32, kind="ExternalInput").ap()
    wk = nc.dram_tensor("wk", [D, KV], F32, kind="ExternalInput").ap()
    wv = nc.dram_tensor("wv", [D, KV], F32, kind="ExternalInput").ap()
    wo = nc.dram_tensor("wo", [KV, D], F32, kind="ExternalInput").ap()
    bases = nc.dram_tensor("bases", [4, D, D], I8, kind="ExternalInput").ap()
    alphas_t = nc.dram_tensor("alphas_t", [D, 4], F32, kind="ExternalInput").ap()
    ident_in = nc.dram_tensor("ident", [128, 128], BF16, kind="ExternalInput").ap()
    out = nc.dram_tensor("out", [OWN, D], F32, kind="ExternalOutput").ap()

    xnT_sp = nc.dram_tensor("xnT_sp", [128, NT, S], BF16, kind="Internal").ap()
    qT_sp = nc.dram_tensor("qT_sp", [128, HEADS, OWN], BF16, kind="Internal").ap()
    kT_sp = nc.dram_tensor("kT_sp", [128, HEADS, S], BF16, kind="Internal").ap()
    v_sp = nc.dram_tensor("v_sp", [128, NT, KV], BF16, kind="Internal").ap()
    wqT_sp = nc.dram_tensor("wqT_sp", [8, 128, NT, 512], BF16, kind="Internal").ap()
    g_sp = nc.dram_tensor("g_sp", [OWN, D], BF16, kind="Internal").ap()
    wo_bf = nc.dram_tensor("wo_bf", [KV, D], BF16, kind="Internal").ap()

    with ExitStack() as gctx:
        GE = gctx.enter_context
        ident_sb = GE(nc.sbuf_tensor("ident_sb", [128, 128], BF16))
        ones1 = GE(nc.sbuf_tensor("ones1", [128, 1], BF16))
        onesr = GE(nc.sbuf_tensor("onesr", [1, 128], F32))
        al_sb = GE(nc.sbuf_tensor("al_sb", [128, 128], F32))
        dum_sb = GE(nc.sbuf_tensor("dum_sb", [128, 1], F32))
        av_sb = GE(nc.sbuf_tensor("av_sb", [128, HEADS * OWN], BF16))

        bar = GE(nc.semaphore("bar"))
        nbar = [0]

        block = GE(nc.Block())

        def barrier():
            nbar[0] += 5
            tgt = nbar[0]
            for dec in (block.sync, block.scalar, block.vector, block.tensor, block.gpsimd):
                @dec
                def _(e):
                    e.sem_inc(bar, 1)
                    e.wait_ge(bar, tgt)

        with ExitStack() as ctx:
            E = ctx.enter_context
            x_sb = [E(nc.sbuf_tensor(f"x_sb{i}", [128, D], F32)) for i in range(2)]
            sq_scr = E(nc.sbuf_tensor("sq_scr", [128, D], F32))
            s1 = [E(nc.sbuf_tensor(f"s1_{i}", [128, 1], F32)) for i in range(2)]
            s2 = [E(nc.sbuf_tensor(f"s2_{i}", [128, 1], F32)) for i in range(2)]
            mu = [E(nc.sbuf_tensor(f"mu_{i}", [128, 1], F32)) for i in range(2)]
            rr = [E(nc.sbuf_tensor(f"rr_{i}", [128, 1], F32)) for i in range(2)]
            nb = [E(nc.sbuf_tensor(f"nb_{i}", [128, 1], F32)) for i in range(2)]
            tp = [E(nc.sbuf_tensor(f"tp_{i}", [128, 1], F32)) for i in range(2)]
            eps_sb = E(nc.sbuf_tensor("eps_sb", [128, 1], F32))
            xn_bf = [E(nc.sbuf_tensor(f"xn_bf{i}", [128, D], BF16)) for i in range(4)]
            stage = [E(nc.sbuf_tensor(f"stage{i}", [128, NT * 512], BF16)) for i in range(2)]
            ps_t = [E(nc.psum_tensor(f"ps_t{i}", [128, 512], BF16)) for i in range(2)]

            ldi = GE(nc.semaphore("ldi"))
            ld0 = GE(nc.semaphore("ld0"))
            ld1 = [GE(nc.semaphore(f"ld1_{i}")) for i in range(2)]
            sp1p = [GE(nc.semaphore(f"sp1_{i}")) for i in range(2)]
            se_sq = GE(nc.semaphore("se_sq"))
            ve_st = GE(nc.semaphore("ve_st"))
            se_r = GE(nc.semaphore("se_r"))
            ve_nb = GE(nc.semaphore("ve_nb"))
            se_nm = GE(nc.semaphore("se_nm"))
            te1 = GE(nc.semaphore("te1"))
            ve_ev = GE(nc.semaphore("ve_ev"))
            sp1 = GE(nc.semaphore("sp1"))

            @block.sync
            def _(e):
                e.dma_start(out=ident_sb[:], in_=ident_in[:, :]).then_inc(ldi, 16)
                for pt in range(32):
                    if pt >= 2:
                        e.wait_ge(se_nm, pt - 1)
                    e.dma_start(out=x_sb[pt % 2][:], in_=x[pt * 128:(pt + 1) * 128, :]).then_inc(ld1[pt % 2], 16)
                    if pt >= 7 and (pt - 7) % 4 == 0:
                        q = (pt - 7) // 4
                        e.wait_ge(ve_ev, 32 * (q + 1))
                        e.dma_start(out=xnT_sp[:, :, q * 512:(q + 1) * 512],
                                    in_=stage[q % 2][:]).then_inc(sp1p[q % 2], 16)
                q = 7
                e.wait_ge(ve_ev, 32 * (q + 1))
                e.dma_start(out=xnT_sp[:, :, q * 512:(q + 1) * 512],
                            in_=stage[q % 2][:]).then_inc(sp1p[q % 2], 16)
                e.wait_ge(sp1p[0], 16 * 4)
                e.wait_ge(sp1p[1], 16 * 4)

            @block.gpsimd
            def _(e):
                e.dma_start(out=wo_bf[:, :], in_=wo[:, :], max_dma_last_dim=8192).then_inc(ld0, 16)
                for op in range(32):
                    e.dma_start(out=al_sb[:, op * 4:(op + 1) * 4],
                                in_=alphas_t[op * 128:(op + 1) * 128, :]).then_inc(ld0, 16)
                e.wait_ge(ld0, 16 * 33)

            @block.vector
            def _(e):
                e.memset(ones1[:], 1.0)
                e.memset(onesr[:], 1.0)
                def evac_quad(q):
                    if q >= 2:
                        e.wait_ge(sp1p[q % 2], 16 * (q // 2))
                    for dt in range(32):
                        gi = q * 32 + dt
                        e.wait_ge(te1, gi + 1)
                        cp = e.tensor_copy(stage[q % 2][:, dt * 512:(dt + 1) * 512],
                                           ps_t[dt % 2][:])
                        if dt < 31:
                            cp.then_inc(ve_ev, 1)
                    e.engine_nop()
                    e.engine_nop().then_inc(ve_ev, 1)

                e.memset(eps_sb[:], EPS)
                for pt in range(32):
                    p = pt % 2
                    e.wait_ge(ld1[pt % 2], 16 * (pt // 2 + 1))
                    e.reduce_sum(s1[p][:], x_sb[p][:], AX.X)
                    e.scalar_tensor_tensor(sq_scr[:], x_sb[p][:], 1.0, x_sb[p][:],
                                           ALU.mult, ALU.mult, accum_out=s2[p][:])
                    e.tensor_scalar_mul(mu[p][:], s1[p][:], 1.0 / D)
                    e.tensor_scalar_mul(nb[p][:], s1[p][:], -1.0 / D)
                    for _i in range(4):
                        e.engine_nop()
                    e.scalar_tensor_tensor(tp[p][:], mu[p][:], mu[p][:], eps_sb[:],
                                           ALU.mult, ALU.subtract)
                    for _i in range(4):
                        e.engine_nop()
                    e.scalar_tensor_tensor(rr[p][:], s2[p][:], 1.0 / D, tp[p][:],
                                           ALU.mult, ALU.subtract)
                    for _i in range(4):
                        e.engine_nop()
                    e.reciprocal(rr[p][:], rr[p][:])
                    for _i in range(3):
                        e.engine_nop()
                    e.engine_nop().then_inc(ve_st, 1)
                    if pt >= 4 and pt % 4 == 0:
                        evac_quad(pt // 4 - 1)
                evac_quad(7)

            @block.scalar
            def _(e):
                for pt in range(32):
                    p = pt % 2
                    e.wait_ge(ve_st, pt + 1)
                    e.activation(rr[p][:], rr[p][:], AF.Sqrt)
                    e.activation(nb[p][:], nb[p][:], AF.Identity, scale=rr[p][:])
                    if pt >= 4:
                        e.wait_ge(te1, 32 * (pt // 4))
                    e.activation(xn_bf[pt % 4][:], x_sb[p][:], AF.Identity,
                                 bias=nb[p][:], scale=rr[p][:])
                    e.copy(out=dum_sb[:], in_=dum_sb[:]).then_inc(se_nm, 1)

            @block.tensor
            def _(e):
                e.wait_ge(ldi, 16)
                for q in range(8):
                    e.wait_ge(se_nm, 4 * (q + 1))
                    for dt in range(32):
                        gi = q * 32 + dt
                        if gi >= 2:
                            e.wait_ge(ve_ev, gi - 1)
                        for j in range(4):
                            mm = e.transpose(ps_t[dt % 2][:, j * 128:(j + 1) * 128],
                                             xn_bf[j][:, dt * 128:(dt + 1) * 128], ident_sb[:])
                        mm.then_inc(te1, 1)

            barrier()

        with ExitStack() as ctx:
            E = ctx.enter_context
            wq_sb = [E(nc.sbuf_tensor(f"wq_sb{i}", [128, D], F32)) for i in range(2)]
            ta = [E(nc.sbuf_tensor(f"ta{i}", [128, 1024], F32)) for i in range(2)]
            tb = [E(nc.sbuf_tensor(f"tb{i}", [128, 1024], F32)) for i in range(2)]
            wq_eff = E(nc.sbuf_tensor("wq_eff", [128, NT * 1024], BF16))
            xc_sb = [E(nc.sbuf_tensor(f"xc_sb{i}", [128, NT * 256], BF16)) for i in range(2)]
            qstg = [E(nc.sbuf_tensor(f"qstg{i}", [128, 8 * 256], BF16)) for i in range(2)]
            ps_q = [E(nc.psum_tensor(f"ps_q{i}", [128, 256], F32)) for i in range(8)]

            ld_wq = [GE(nc.semaphore(f"ld_wq{i}")) for i in range(2)]
            ve_ab = GE(nc.semaphore("ve_ab"))
            ve_eff = GE(nc.semaphore("ve_eff"))
            ld_xc = [GE(nc.semaphore(f"ld_xc{i}")) for i in range(2)]
            te2a = GE(nc.semaphore("te2a"))
            se2a = GE(nc.semaphore("se2a"))
            sp2a = [GE(nc.semaphore(f"sp2a{i}")) for i in range(2)]

            @block.scalar
            def _(e):
                for it in range(32):
                    if it >= 2:
                        e.wait_ge(ve_ab, it - 1)
                    e.dma_start(out=wq_sb[it % 2][:], in_=wq[it * 128:(it + 1) * 128, :]).then_inc(ld_wq[it % 2], 16)
                for sc in range(8):
                    e.wait_ge(te2a, sc + 1)
                    if sc >= 2:
                        e.wait_ge(sp2a[sc % 2], 16 * (sc // 2))
                    for mt in range(8):
                        cp = e.copy(out=qstg[sc % 2][:, mt * 256:(mt + 1) * 256],
                                    in_=ps_q[mt][:])
                        if mt < 7:
                            cp.then_inc(se2a, 1)
                    e.copy(out=dum_sb[:], in_=dum_sb[:]).then_inc(se2a, 1)

            @block.vector
            def _(e):
                def hj(apx):
                    return apx.rearrange("p (h j) -> p h j", h=8, j=128)
                for it in range(32):
                    p = it % 2
                    e.wait_ge(ld_wq[it % 2], 16 * (it // 2 + 1))
                    wv_ = wq_sb[p][:].rearrange("p (h gj) -> p h gj", h=8, gj=512)
                    e.tensor_tensor(hj(ta[p][:]), wv_[:, :, 0:128], wv_[:, :, 128:256], ALU.add)
                    e.tensor_tensor(hj(tb[p][:]), wv_[:, :, 256:384], wv_[:, :, 384:512],
                                    ALU.add).then_inc(ve_ab, 1)
                    if it >= 1:
                        pp = (it - 1) % 2
                        e.tensor_tensor(hj(wq_eff[:, (it - 1) * 1024: it * 1024]),
                                        hj(ta[pp][:]), hj(tb[pp][:]), ALU.add)
                        e.engine_nop()
                        e.engine_nop().then_inc(ve_eff, 1)
                e.engine_nop()
                e.tensor_tensor(hj(wq_eff[:, 31 * 1024: 32 * 1024]),
                                hj(ta[1][:]), hj(tb[1][:]), ALU.add)
                e.engine_nop()
                e.engine_nop().then_inc(ve_eff, 1)

            @block.sync
            def _(e):
                def spill(sc):
                    e.wait_ge(se2a, 8 * (sc + 1))
                    e.dma_start(out=qT_sp[:, :, sc * 256:(sc + 1) * 256],
                                in_=qstg[sc % 2][:]).then_inc(sp2a[sc % 2], 16)
                for sc in range(8):
                    if sc >= 2:
                        e.wait_ge(te2a, sc - 1)
                    e.dma_start(out=xc_sb[sc % 2][:],
                                in_=xnT_sp[:, :, sc * 256:(sc + 1) * 256]).then_inc(ld_xc[sc % 2], 16)
                    if sc >= 2:
                        spill(sc - 2)
                spill(6)
                spill(7)
                e.wait_ge(sp2a[0], 16 * 4)
                e.wait_ge(sp2a[1], 16 * 4)

            @block.tensor
            def _(e):
                for sc in range(8):
                    e.wait_ge(ld_xc[sc % 2], 16 * (sc // 2 + 1))
                    if sc >= 1:
                        e.wait_ge(se2a, 8 * sc)
                    for it in range(32):
                        if sc == 0:
                            e.wait_ge(ve_eff, min(it + 2, 32))
                        for mt in range(8):
                            mm = e.matmul(ps_q[mt][:],
                                          wq_eff[:, it * 1024 + mt * 128: it * 1024 + (mt + 1) * 128],
                                          xc_sb[sc % 2][:, it * 256:(it + 1) * 256],
                                          start=(it == 0), stop=(it == 31))
                    mm.then_inc(te2a, 1)

            barrier()

        def proj_kv(which):
            with ExitStack() as ctx:
                E = ctx.enter_context
                w_sb = E(nc.sbuf_tensor(f"w_sb_{which}", [128, NT * 1024], BF16))
                c_sb = [E(nc.sbuf_tensor(f"c_sb_{which}{i}", [128, NT * 512], BF16)) for i in range(2)]
                kstg = [E(nc.sbuf_tensor(f"stg_{which}{i}", [128, 8 * 512], BF16)) for i in range(2)]
                ps = [E(nc.psum_tensor(f"ps_{which}{i}", [128, 512], F32)) for i in range(2)]

                ld_w = GE(nc.semaphore(f"ld_w{which}"))
                ld_c = [GE(nc.semaphore(f"ld_c{which}{i}")) for i in range(2)]
                te = GE(nc.semaphore(f"te2b{which}"))
                se = GE(nc.semaphore(f"se2b{which}"))
                sp = [GE(nc.semaphore(f"sp2b{which}{i}")) for i in range(2)]
                wsrc = wk if which == "k" else wv
                n_per_sc = 8 if which == "k" else 4

                @block.gpsimd
                def _(e):
                    for it in range(32):
                        e.dma_start(out=w_sb[:, it * 1024:(it + 1) * 1024],
                                    in_=wsrc[it * 128:(it + 1) * 128, :]).then_inc(ld_w, 16)

                @block.sync
                def _(e):
                    def spill_k(sc):
                        e.wait_ge(se, 8 * (sc + 1))
                        e.dma_start(out=kT_sp[:, :, sc * 512:(sc + 1) * 512],
                                    in_=kstg[sc % 2][:]).then_inc(sp[sc % 2], 16)

                    def spill_v(gi):
                        e.wait_ge(se, 2 * (gi + 1))
                        e.dma_start(out=v_sp[:, gi, :],
                                    in_=kstg[gi % 2][:, 0:1024]).then_inc(sp[gi % 2], 16)

                    for sc in range(8):
                        if sc >= 2:
                            if which == "k":
                                spill_k(sc - 2)
                            else:
                                for gi in range((sc - 2) * 4, (sc - 1) * 4):
                                    spill_v(gi)
                            e.wait_ge(te, (sc - 1) * n_per_sc)
                        e.dma_start(out=c_sb[sc % 2][:],
                                    in_=xnT_sp[:, :, sc * 512:(sc + 1) * 512]).then_inc(ld_c[sc % 2], 16)
                    if which == "k":
                        spill_k(6)
                        spill_k(7)
                        e.wait_ge(sp[0], 16 * 4)
                        e.wait_ge(sp[1], 16 * 4)
                    else:
                        for gi in range(24, 32):
                            spill_v(gi)
                        e.wait_ge(sp[0], 16 * 16)
                        e.wait_ge(sp[1], 16 * 16)

                if which == "k":
                    @block.tensor
                    def _(e):
                        e.wait_ge(ld_w, 16 * 32)
                        for sc in range(8):
                            e.wait_ge(ld_c[sc % 2], 16 * (sc // 2 + 1))
                            for mt in range(8):
                                gi = sc * 8 + mt
                                if gi >= 2:
                                    e.wait_ge(se, gi - 1)
                                for it in range(32):
                                    mm = e.matmul(ps[mt % 2][:],
                                                  w_sb[:, it * 1024 + mt * 128: it * 1024 + (mt + 1) * 128],
                                                  c_sb[sc % 2][:, it * 512:(it + 1) * 512],
                                                  start=(it == 0), stop=(it == 31))
                                mm.then_inc(te, 1)

                    @block.scalar
                    def _(e):
                        for sc in range(8):
                            if sc >= 2:
                                e.wait_ge(sp[sc % 2], 16 * (sc // 2))
                            for mt in range(8):
                                gi = sc * 8 + mt
                                e.wait_ge(te, gi + 1)
                                cp = e.copy(out=kstg[sc % 2][:, mt * 512:(mt + 1) * 512],
                                            in_=ps[mt % 2][:])
                                if mt < 7:
                                    cp.then_inc(se, 1)
                            e.copy(out=dum_sb[:], in_=dum_sb[:]).then_inc(se, 1)
                else:
                    @block.tensor
                    def _(e):
                        e.wait_ge(ld_w, 16 * 32)
                        for sc in range(8):
                            e.wait_ge(ld_c[sc % 2], 16 * (sc // 2 + 1))
                            for smt in range(4):
                                gi = sc * 4 + smt
                                if gi >= 1:
                                    e.wait_ge(se, 2 * gi)
                                for nsl in range(2):
                                    for it in range(32):
                                        mm = e.matmul(ps[nsl][:],
                                                      c_sb[sc % 2][:, it * 512 + smt * 128: it * 512 + (smt + 1) * 128],
                                                      w_sb[:, it * 1024 + nsl * 512: it * 1024 + (nsl + 1) * 512],
                                                      start=(it == 0), stop=(it == 31))
                                mm.then_inc(te, 1)

                    @block.scalar
                    def _(e):
                        for sc in range(8):
                            for smt in range(4):
                                gi = sc * 4 + smt
                                e.wait_ge(te, gi + 1)
                                if gi >= 2:
                                    e.wait_ge(sp[gi % 2], 16 * (gi // 2))
                                e.copy(out=kstg[smt % 2][:, 0:512], in_=ps[0][:]).then_inc(se, 1)
                                e.copy(out=kstg[smt % 2][:, 512:1024], in_=ps[1][:])
                                e.copy(out=dum_sb[:], in_=dum_sb[:]).then_inc(se, 1)

                barrier()

        proj_kv("k")
        proj_kv("v")

        with ExitStack() as ctx:
            E = ctx.enter_context
            b_sb = [E(nc.sbuf_tensor(f"b_sb{i}", [128, D], I8)) for i in range(2)]
            t0 = E(nc.sbuf_tensor("t0", [128, D], BF16))
            t1 = E(nc.sbuf_tensor("t1", [128, D], BF16))
            wq_ring = [E(nc.sbuf_tensor(f"wq_ring{i}", [128, D], BF16)) for i in range(8)]
            wstg = [E(nc.sbuf_tensor(f"wstg{i}", [128, NT * 512], BF16)) for i in range(2)]
            ps_t2 = [E(nc.psum_tensor(f"ps_t2{i}", [128, 512], BF16)) for i in range(2)]

            ld_b = [GE(nc.semaphore(f"ld_b{i}")) for i in range(2)]
            se_t01 = GE(nc.semaphore("se_t01"))
            ve_b23 = GE(nc.semaphore("ve_b23"))
            ve_acc = GE(nc.semaphore("ve_acc"))
            te2c = GE(nc.semaphore("te2c"))
            ve_ev2 = GE(nc.semaphore("ve_ev2"))
            sp2c = [GE(nc.semaphore(f"sp2c{i}")) for i in range(2)]

            @block.sync
            def _(e):
                for op in range(32):
                    for k in range(4):
                        gi = op * 4 + k
                        if gi >= 2:
                            opp, kk = (gi - 2) // 4, (gi - 2) % 4
                            if kk < 2:
                                e.wait_ge(se_t01, opp * 2 + kk + 1)
                            else:
                                e.wait_ge(ve_b23, opp * 2 + kk - 1)
                        e.dma_start(out=b_sb[gi % 2][:],
                                    in_=bases[k, op * 128:(op + 1) * 128, :]).then_inc(ld_b[gi % 2], 16)

            def emit_acts(e, og):
                for j in range(4):
                    op = og * 4 + j
                    if op >= 1:
                        e.wait_ge(ve_acc, 2 * op - 1)
                    e.wait_ge(ld_b[0], 16 * (2 * op + 1))
                    e.activation(t0[:], b_sb[(op * 4) % 2][:], AF.Identity,
                                 scale=al_sb[:, op * 4: op * 4 + 1]).then_inc(se_t01, 1)
                    e.wait_ge(ld_b[1], 16 * (2 * op + 1))
                    e.activation(t1[:], b_sb[(op * 4 + 1) % 2][:], AF.Identity,
                                 scale=al_sb[:, op * 4 + 1: op * 4 + 2])
                    e.copy(out=dum_sb[:], in_=dum_sb[:]).then_inc(se_t01, 1)

            def emit_spill(e, og):
                e.wait_ge(ve_ev2, 32 * (og + 1))
                e.dma_start(out=wqT_sp[og, :, :, :], in_=wstg[og % 2][:]).then_inc(sp2c[og % 2], 16)

            @block.scalar
            def _(e):
                emit_acts(e, 0)
                emit_acts(e, 1)
                for og in range(2, 8):
                    emit_acts(e, og)
                    emit_spill(e, og - 2)
                emit_spill(e, 6)
                emit_spill(e, 7)
                e.wait_ge(sp2c[0], 16 * 4)
                e.wait_ge(sp2c[1], 16 * 4)

            def emit_builds(e, og):
                for j in range(4):
                    op = og * 4 + j
                    r = wq_ring[op % 8]
                    if og >= 2:
                        e.wait_ge(te2c, (og - 1) * 32)
                    e.wait_ge(se_t01, op * 2 + 2)
                    e.tensor_tensor(r[:], t0[:], t1[:], ALU.add).then_inc(ve_acc, 1)
                    e.wait_ge(ld_b[0], 16 * (2 * op + 2))
                    e.engine_nop()
                    e.scalar_tensor_tensor(r[:], b_sb[(op * 4 + 2) % 2][:],
                                           al_sb[:, op * 4 + 2: op * 4 + 3], r[:],
                                           ALU.mult, ALU.add).then_inc(ve_b23, 1)
                    e.wait_ge(ld_b[1], 16 * (2 * op + 2))
                    e.engine_nop()
                    e.scalar_tensor_tensor(r[:], b_sb[(op * 4 + 3) % 2][:],
                                           al_sb[:, op * 4 + 3: op * 4 + 4], r[:],
                                           ALU.mult, ALU.add).then_inc(ve_b23, 1)
                    e.engine_nop().then_inc(ve_acc, 1)

            def emit_evacs(e, og):
                if og >= 2:
                    e.wait_ge(sp2c[og % 2], 16 * (og // 2))
                for it in range(32):
                    gi = og * 32 + it
                    e.wait_ge(te2c, gi + 1)
                    cp = e.tensor_copy(wstg[og % 2][:, it * 512:(it + 1) * 512],
                                       ps_t2[it % 2][:])
                    if it < 31:
                        cp.then_inc(ve_ev2, 1)
                e.engine_nop()
                e.engine_nop().then_inc(ve_ev2, 1)

            @block.vector
            def _(e):
                emit_builds(e, 0)
                emit_builds(e, 1)
                for og in range(2, 8):
                    emit_evacs(e, og - 2)
                    emit_builds(e, og)
                emit_evacs(e, 6)
                emit_evacs(e, 7)

            @block.tensor
            def _(e):
                for og in range(8):
                    e.wait_ge(ve_acc, 8 * (og + 1))
                    for it in range(32):
                        gi = og * 32 + it
                        if gi >= 2:
                            e.wait_ge(ve_ev2, gi - 1)
                        for j in range(4):
                            mm = e.transpose(ps_t2[it % 2][:, j * 128:(j + 1) * 128],
                                             wq_ring[(og * 4 + j) % 8][:, it * 128:(it + 1) * 128],
                                             ident_sb[:])
                        mm.then_inc(te2c, 1)

            barrier()

        with ExitStack() as ctx:
            E = ctx.enter_context
            kt_sb = E(nc.sbuf_tensor("kt_sb", [128, 4 * S], BF16))
            vg_sb = E(nc.sbuf_tensor("vg_sb", [128, NT * 512], BF16))
            qt_sb = E(nc.sbuf_tensor("qt_sb", [128, 4 * OWN], BF16))
            expT = [E(nc.sbuf_tensor(f"expT{i}", [128, 1024], BF16)) for i in range(2)]
            rec_sb = E(nc.sbuf_tensor("rec_sb", [1, 512], F32))
            bc_sb = E(nc.sbuf_tensor("bc_sb", [128, 512], F32))
            ps_s2 = [E(nc.psum_tensor(f"ps_s2{i}", [128, 1024], F32)) for i in range(2)]
            ps_pv = [E(nc.psum_tensor(f"ps_pv{i}", [128, 512], F32)) for i in range(2)]
            ps_dn = [E(nc.psum_tensor(f"ps_dn{i}", [1, 512], F32)) for i in range(2)]

            ld3 = GE(nc.semaphore("ld3"))
            te_s = GE(nc.semaphore("te_s"))
            se_e = GE(nc.semaphore("se_e"))
            te_pvk = GE(nc.semaphore("te_pvk"))
            ve_r = GE(nc.semaphore("ve_r"))
            te_bc = GE(nc.semaphore("te_bc"))
            se_bc = GE(nc.semaphore("se_bc"))
            ve_av = GE(nc.semaphore("ve_av"))

            @block.sync
            def _(e):
                for hg in range(2):
                    if hg == 1:
                        e.wait_ge(te_bc, 16)
                    for hh in range(4):
                        e.dma_start(out=kt_sb[:, hh * S:(hh + 1) * S],
                                    in_=kT_sp[:, hg * 4 + hh, :]).then_inc(ld3, 16)
                        e.dma_start(out=qt_sb[:, hh * OWN:(hh + 1) * OWN],
                                    in_=qT_sp[:, hg * 4 + hh, :]).then_inc(ld3, 16)
                    e.dma_start(out=vg_sb[:], in_=v_sp[:, :, hg * 512:(hg + 1) * 512]).then_inc(ld3, 16)

            @block.tensor
            def _(e):
                for hg in range(2):
                    e.wait_ge(ld3, 16 * 9 * (hg + 1))
                    for hh in range(4):
                        for tc in range(4):
                            m = (hg * 4 + hh) * 4 + tc
                            if m >= 2:
                                e.wait_ge(ve_av, m - 1)
                            if m >= 1:
                                e.wait_ge(se_bc, m)
                            for kp in range(16):
                                g = m * 16 + kp
                                if g >= 2:
                                    e.wait_ge(se_e, g - 1)
                                for j in range(2):
                                    kt = 2 * kp + j
                                    mm = e.matmul(ps_s2[kp % 2][:, j * 512:(j + 1) * 512],
                                                  kt_sb[:, hh * S + kt * 128: hh * S + (kt + 1) * 128],
                                                  qt_sb[:, hh * OWN + tc * 512: hh * OWN + (tc + 1) * 512],
                                                  start=True, stop=True)
                                mm.then_inc(te_s, 1)
                                if kp >= 1:
                                    kq = kp - 1
                                    e.wait_ge(se_e, m * 16 + kq + 1)
                                    for j in range(2):
                                        kt = 2 * kq + j
                                        e.matmul(ps_pv[m % 2][:],
                                                 vg_sb[:, kt * 512 + hh * 128: kt * 512 + (hh + 1) * 128],
                                                 expT[kq % 2][:, j * 512:(j + 1) * 512],
                                                 start=(kt == 0), stop=(kt == 31))
                                        mm = e.matmul(ps_dn[m % 2][:], ones1[:],
                                                      expT[kq % 2][:, j * 512:(j + 1) * 512],
                                                      start=(kt == 0), stop=(kt == 31))
                                    mm.then_inc(te_pvk, 1)
                            kq = 15
                            e.wait_ge(se_e, m * 16 + kq + 1)
                            for j in range(2):
                                kt = 2 * kq + j
                                e.matmul(ps_pv[m % 2][:],
                                         vg_sb[:, kt * 512 + hh * 128: kt * 512 + (hh + 1) * 128],
                                         expT[kq % 2][:, j * 512:(j + 1) * 512],
                                         start=(kt == 0), stop=(kt == 31))
                                mm = e.matmul(ps_dn[m % 2][:], ones1[:],
                                              expT[kq % 2][:, j * 512:(j + 1) * 512],
                                              start=(kt == 0), stop=(kt == 31))
                            mm.then_inc(te_pvk, 1)
                            e.wait_ge(ve_r, m + 1)
                            e.matmul(ps_s2[0][:, 0:512], onesr[:], rec_sb[:],
                                     start=True, stop=True).then_inc(te_bc, 1)

            @block.scalar
            def _(e):
                for m in range(32):
                    for kp in range(16):
                        g = m * 16 + kp
                        e.wait_ge(te_s, g + 1)
                        if g >= 2:
                            e.wait_ge(te_pvk, g - 1)
                        e.activation(expT[kp % 2][:], ps_s2[kp % 2][:], AF.Exp,
                                     scale=SC)
                        e.copy(out=dum_sb[:], in_=dum_sb[:]).then_inc(se_e, 1)
                    e.wait_ge(te_bc, m + 1)
                    if m >= 1:
                        e.wait_ge(ve_av, m)
                    e.copy(out=bc_sb[:], in_=ps_s2[0][:, 0:512])
                    e.copy(out=dum_sb[:], in_=dum_sb[:]).then_inc(se_bc, 1)

            @block.vector
            def _(e):
                for m in range(32):
                    hd = m // 4
                    tc = m % 4
                    e.wait_ge(te_pvk, 16 * (m + 1))
                    if m >= 1:
                        e.wait_ge(te_bc, m)
                    e.reciprocal(rec_sb[:], ps_dn[m % 2][0:1, :])
                    e.engine_nop()
                    e.engine_nop().then_inc(ve_r, 1)
                    e.wait_ge(se_bc, m + 1)
                    e.scalar_tensor_tensor(av_sb[:, hd * OWN + tc * 512: hd * OWN + (tc + 1) * 512],
                                           ps_pv[m % 2][:], 1.0, bc_sb[:],
                                           ALU.mult, ALU.mult).then_inc(ve_av, 1)

            barrier()

        with ExitStack() as ctx:
            E = ctx.enter_context
            xh_sb = E(nc.sbuf_tensor("xh_sb", [128, NT * 1024], BF16))
            wqT_sb = [E(nc.sbuf_tensor(f"wqT_sb{i}", [128, NT * 512], BF16)) for i in range(2)]
            g_st = [E(nc.sbuf_tensor(f"g_st{i}", [128, 512], BF16)) for i in range(2)]
            ps_f = [E(nc.psum_tensor(f"ps_f{i}", [128, 512], F32)) for i in range(2)]

            ld4x = GE(nc.semaphore("ld4x"))
            ld4w = [GE(nc.semaphore(f"ld4w{i}")) for i in range(2)]
            te4 = GE(nc.semaphore("te4"))
            se4 = GE(nc.semaphore("se4"))
            sp4 = [GE(nc.semaphore(f"sp4_{i}")) for i in range(2)]

            @block.sync
            def _(e):
                def spill_g(gj):
                    rh2, oc2, mt2 = gj // 64, (gj // 8) % 8, gj % 8
                    e.wait_ge(se4, gj + 1)
                    e.dma_start(out=g_sp[rh2 * 1024 + mt2 * 128: rh2 * 1024 + (mt2 + 1) * 128,
                                         oc2 * 512:(oc2 + 1) * 512],
                                in_=g_st[gj % 2][:]).then_inc(sp4[gj % 2], 16)
                e.dma_start(out=xh_sb[:], in_=xnT_sp[:, :, 0:1024]).then_inc(ld4x, 16)
                for gj in range(64):
                    spill_g(gj)
                e.wait_ge(te4, 64)
                e.dma_start(out=xh_sb[:], in_=xnT_sp[:, :, 1024:2048]).then_inc(ld4x, 16)
                for gj in range(64, 128):
                    spill_g(gj)
                e.wait_ge(sp4[0], 16 * 64)
                e.wait_ge(sp4[1], 16 * 64)

            @block.gpsimd
            def _(e):
                for rh in range(2):
                    for oc in range(8):
                        gi = rh * 8 + oc
                        if gi >= 2:
                            e.wait_ge(te4, (gi - 1) * 8)
                        e.dma_start(out=wqT_sb[gi % 2][:], in_=wqT_sp[oc, :, :, :]).then_inc(ld4w[gi % 2], 16)

            @block.tensor
            def _(e):
                for rh in range(2):
                    e.wait_ge(ld4x, 16 * (rh + 1))
                    for oc in range(8):
                        gi = rh * 8 + oc
                        e.wait_ge(ld4w[gi % 2], 16 * (gi // 2 + 1))
                        for mt in range(8):
                            gj = gi * 8 + mt
                            if gj >= 2:
                                e.wait_ge(se4, gj - 1)
                            for it in range(32):
                                mm = e.matmul(ps_f[mt % 2][:],
                                              xh_sb[:, it * 1024 + mt * 128: it * 1024 + (mt + 1) * 128],
                                              wqT_sb[gi % 2][:, it * 512:(it + 1) * 512],
                                              start=(it == 0), stop=(it == 31))
                            mm.then_inc(te4, 1)

            @block.scalar
            def _(e):
                for rh in range(2):
                    for oc in range(8):
                        for mt in range(8):
                            gj = (rh * 8 + oc) * 8 + mt
                            e.wait_ge(te4, gj + 1)
                            if gj >= 2:
                                e.wait_ge(sp4[gj % 2], 16 * (gj // 2))
                            e.activation(g_st[gj % 2][:], ps_f[mt % 2][:], AF.Silu)
                            e.copy(out=dum_sb[:], in_=dum_sb[:]).then_inc(se4, 1)

            barrier()

        with ExitStack() as ctx:
            E = ctx.enter_context
            wo_sb = E(nc.sbuf_tensor("wo_sb", [128, 8 * D], BF16))
            g_in = [E(nc.sbuf_tensor(f"g_in{i}", [128, 512], BF16)) for i in range(2)]
            x_in = [E(nc.sbuf_tensor(f"x_in{i}", [128, 512], F32)) for i in range(2)]
            t_sb = [E(nc.sbuf_tensor(f"t_sb{i}", [128, 512], F32)) for i in range(2)]
            o_st = [E(nc.sbuf_tensor(f"o_st{i}", [128, 512], F32)) for i in range(2)]
            ps_o = [E(nc.psum_tensor(f"ps_o{i}", [128, 512], F32)) for i in range(2)]

            ld5w = GE(nc.semaphore("ld5w"))
            ldg = [GE(nc.semaphore(f"ldg{i}")) for i in range(2)]
            ldx = [GE(nc.semaphore(f"ldx{i}")) for i in range(2)]
            te5 = GE(nc.semaphore("te5"))
            ve5s = GE(nc.semaphore("ve5s"))
            ve5t = GE(nc.semaphore("ve5t"))
            sp5 = [GE(nc.semaphore(f"sp5_{i}")) for i in range(2)]

            @block.gpsimd
            def _(e):
                for kt in range(8):
                    e.dma_start(out=wo_sb[:, kt * D:(kt + 1) * D],
                                in_=wo_bf[kt * 128:(kt + 1) * 128, :]).then_inc(ld5w, 16)

            @block.sync
            def _(e):
                for i in range(128):
                    mt, dc = i // 8, i % 8
                    if i >= 2:
                        e.wait_ge(ve5s, i - 1)
                        e.wait_ge(ve5t, i - 1)
                    e.dma_start(out=g_in[i % 2][:],
                                in_=g_sp[mt * 128:(mt + 1) * 128, dc * 512:(dc + 1) * 512]).then_inc(ldg[i % 2], 16)
                    e.dma_start(out=x_in[i % 2][:],
                                in_=x[mt * 128:(mt + 1) * 128, dc * 512:(dc + 1) * 512]).then_inc(ldx[i % 2], 16)

            @block.tensor
            def _(e):
                e.wait_ge(ld5w, 16 * 8)
                for i in range(128):
                    mt, dc = i // 8, i % 8
                    if i >= 2:
                        e.wait_ge(ve5s, i - 1)
                    for kt in range(8):
                        mm = e.matmul(ps_o[i % 2][:],
                                      av_sb[:, kt * OWN + mt * 128: kt * OWN + (mt + 1) * 128],
                                      wo_sb[:, kt * D + dc * 512: kt * D + (dc + 1) * 512],
                                      start=(kt == 0), stop=(kt == 7))
                    mm.then_inc(te5, 1)

            @block.vector
            def _(e):
                for i in range(128):
                    e.wait_ge(te5, i + 1)
                    e.wait_ge(ldg[i % 2], 16 * (i // 2 + 1))
                    e.scalar_tensor_tensor(t_sb[i % 2][:], ps_o[i % 2][:], 1.0, g_in[i % 2][:],
                                           ALU.mult, ALU.mult).then_inc(ve5s, 1)
                    if i >= 1:
                        j = i - 1
                        e.wait_ge(ldx[j % 2], 16 * (j // 2 + 1))
                        if j >= 2:
                            e.wait_ge(sp5[j % 2], 16 * (j // 2))
                        e.tensor_tensor(o_st[j % 2][:], t_sb[j % 2][:], x_in[j % 2][:],
                                        ALU.add)
                        e.engine_nop().then_inc(ve5t, 1)
                j = 127
                e.engine_nop()
                e.wait_ge(ldx[j % 2], 16 * (j // 2 + 1))
                e.wait_ge(sp5[j % 2], 16 * (j // 2))
                e.tensor_tensor(o_st[j % 2][:], t_sb[j % 2][:], x_in[j % 2][:],
                                ALU.add)
                e.engine_nop().then_inc(ve5t, 1)

            @block.scalar
            def _(e):
                for i in range(128):
                    mt, dc = i // 8, i % 8
                    e.wait_ge(ve5t, i + 1)
                    e.dma_start(out=out[mt * 128:(mt + 1) * 128, dc * 512:(dc + 1) * 512],
                                in_=o_st[i % 2][:]).then_inc(sp5[i % 2], 16)
                e.wait_ge(sp5[0], 16 * 64)
                e.wait_ge(sp5[1], 16 * 64)

    return nc


_NC_CACHE = None


def kernel(L0_attn_x, L0_attn_wq, L0_attn_wk, L0_attn_wv, L0_attn_wo, L0_ffn_bases, L0_ffn_alphas):
    global _NC_CACHE
    x = np.ascontiguousarray(np.asarray(L0_attn_x, dtype=np.float32))
    wq = np.ascontiguousarray(np.asarray(L0_attn_wq, dtype=np.float32))
    wk = np.ascontiguousarray(np.asarray(L0_attn_wk, dtype=np.float32))
    wv = np.ascontiguousarray(np.asarray(L0_attn_wv, dtype=np.float32))
    wo = np.ascontiguousarray(np.asarray(L0_attn_wo, dtype=np.float32))
    bases = np.ascontiguousarray(np.asarray(L0_ffn_bases, dtype=np.int8))
    alphas_t = np.ascontiguousarray(np.asarray(L0_ffn_alphas, dtype=np.float32).reshape(4, D).T)
    ident = np.eye(128, dtype=ml_dtypes.bfloat16)

    if _NC_CACHE is None:
        _NC_CACHE = build_nc()
    nc = _NC_CACHE

    in_maps = []
    for c in range(8):
        b, h = c // 2, c % 2
        xb = x[b]
        xp = np.ascontiguousarray(
            np.concatenate([xb[h * OWN:(h + 1) * OWN], xb[(1 - h) * OWN:(2 - h) * OWN]], axis=0))
        in_maps.append({"x": xp, "wq": wq, "wk": wk, "wv": wv, "wo": wo,
                        "bases": bases, "alphas_t": alphas_t, "ident": ident})

    res = run_bass_kernel_spmd(nc, in_maps, core_ids=list(range(8)))

    outa = np.empty((4, S, D), np.float32)
    for c in range(8):
        b, h = c // 2, c % 2
        outa[b, h * OWN:(h + 1) * OWN, :] = res.results[c]["out"]
    return outa


# version 09
# speedup vs baseline: 1.0171x; 1.0171x over previous
import numpy as np
import ml_dtypes
import concourse.bass as bass
import concourse.mybir as mybir
from concourse.bass_utils import run_bass_kernel_spmd
from contextlib import ExitStack

F32 = mybir.dt.float32
BF16 = mybir.dt.bfloat16
I8 = mybir.dt.int8
AF = mybir.ActivationFunctionType
ALU = mybir.AluOpType
AX = mybir.AxisListType

D = 4096
S = 4096
OWN = 2048
NT = D // 128
HEADS = 8
KV = 1024
SC = 0.08838834764831845
EPS = 1e-5


def build_nc():
    nc = bass.Bass("TRN2", target_bir_lowering=False)

    x = nc.dram_tensor("x", [S, D], F32, kind="ExternalInput").ap()
    wq = nc.dram_tensor("wq", [D, D], F32, kind="ExternalInput").ap()
    wk = nc.dram_tensor("wk", [D, KV], F32, kind="ExternalInput").ap()
    wv = nc.dram_tensor("wv", [D, KV], F32, kind="ExternalInput").ap()
    wo = nc.dram_tensor("wo", [KV, D], F32, kind="ExternalInput").ap()
    bases = nc.dram_tensor("bases", [4, D, D], I8, kind="ExternalInput").ap()
    alphas_t = nc.dram_tensor("alphas_t", [D, 4], F32, kind="ExternalInput").ap()
    ident_in = nc.dram_tensor("ident", [128, 128], BF16, kind="ExternalInput").ap()
    out = nc.dram_tensor("out", [OWN, D], F32, kind="ExternalOutput").ap()

    xnT_sp = nc.dram_tensor("xnT_sp", [128, NT, S], BF16, kind="Internal").ap()
    qT_sp = nc.dram_tensor("qT_sp", [128, HEADS, OWN], BF16, kind="Internal").ap()
    kT_sp = nc.dram_tensor("kT_sp", [128, HEADS, S], BF16, kind="Internal").ap()
    v_sp = nc.dram_tensor("v_sp", [128, NT, KV], BF16, kind="Internal").ap()
    wqT_sp = nc.dram_tensor("wqT_sp", [8, 128, NT, 512], BF16, kind="Internal").ap()
    g_sp = nc.dram_tensor("g_sp", [OWN, D], BF16, kind="Internal").ap()
    wo_bf = nc.dram_tensor("wo_bf", [KV, D], BF16, kind="Internal").ap()
    bases_bf = nc.dram_tensor("bases_bf", [4, D, D], BF16, kind="Internal").ap()

    with ExitStack() as gctx:
        GE = gctx.enter_context
        ident_sb = GE(nc.sbuf_tensor("ident_sb", [128, 128], BF16))
        ones1 = GE(nc.sbuf_tensor("ones1", [128, 1], BF16))
        onesr = GE(nc.sbuf_tensor("onesr", [1, 128], F32))
        al_sb = GE(nc.sbuf_tensor("al_sb", [128, 128], F32))
        dum_sb = GE(nc.sbuf_tensor("dum_sb", [128, 1], F32))
        av_sb = GE(nc.sbuf_tensor("av_sb", [128, HEADS * OWN], BF16))

        bar = GE(nc.semaphore("bar"))
        nbar = [0]

        block = GE(nc.Block())

        def barrier():
            nbar[0] += 5
            tgt = nbar[0]
            for dec in (block.sync, block.scalar, block.vector, block.tensor, block.gpsimd):
                @dec
                def _(e):
                    e.sem_inc(bar, 1)
                    e.wait_ge(bar, tgt)

        with ExitStack() as ctx:
            E = ctx.enter_context
            x_sb = [E(nc.sbuf_tensor(f"x_sb{i}", [128, D], F32)) for i in range(2)]
            sq_scr = E(nc.sbuf_tensor("sq_scr", [128, D], F32))
            s1 = [E(nc.sbuf_tensor(f"s1_{i}", [128, 1], F32)) for i in range(2)]
            s2 = [E(nc.sbuf_tensor(f"s2_{i}", [128, 1], F32)) for i in range(2)]
            mu = [E(nc.sbuf_tensor(f"mu_{i}", [128, 1], F32)) for i in range(2)]
            rr = [E(nc.sbuf_tensor(f"rr_{i}", [128, 1], F32)) for i in range(2)]
            nb = [E(nc.sbuf_tensor(f"nb_{i}", [128, 1], F32)) for i in range(2)]
            tp = [E(nc.sbuf_tensor(f"tp_{i}", [128, 1], F32)) for i in range(2)]
            eps_sb = E(nc.sbuf_tensor("eps_sb", [128, 1], F32))
            xn_bf = [E(nc.sbuf_tensor(f"xn_bf{i}", [128, D], BF16)) for i in range(4)]
            stage = [E(nc.sbuf_tensor(f"stage{i}", [128, NT * 512], BF16)) for i in range(2)]
            ps_t = [E(nc.psum_tensor(f"ps_t{i}", [128, 512], BF16)) for i in range(2)]

            ldi = GE(nc.semaphore("ldi"))
            ld0 = GE(nc.semaphore("ld0"))
            ld1 = [GE(nc.semaphore(f"ld1_{i}")) for i in range(2)]
            sp1p = [GE(nc.semaphore(f"sp1_{i}")) for i in range(2)]
            se_sq = GE(nc.semaphore("se_sq"))
            ve_st = GE(nc.semaphore("ve_st"))
            se_r = GE(nc.semaphore("se_r"))
            ve_nb = GE(nc.semaphore("ve_nb"))
            se_nm = GE(nc.semaphore("se_nm"))
            te1 = GE(nc.semaphore("te1"))
            ve_ev = GE(nc.semaphore("ve_ev"))
            sp1 = GE(nc.semaphore("sp1"))

            @block.sync
            def _(e):
                e.dma_start(out=ident_sb[:], in_=ident_in[:, :]).then_inc(ldi, 16)
                for pt in range(32):
                    if pt >= 2:
                        e.wait_ge(se_nm, pt - 1)
                    e.dma_start(out=x_sb[pt % 2][:], in_=x[pt * 128:(pt + 1) * 128, :]).then_inc(ld1[pt % 2], 16)
                    if pt >= 7 and (pt - 7) % 4 == 0:
                        q = (pt - 7) // 4
                        e.wait_ge(ve_ev, 32 * (q + 1))
                        e.dma_start(out=xnT_sp[:, :, q * 512:(q + 1) * 512],
                                    in_=stage[q % 2][:]).then_inc(sp1p[q % 2], 16)
                q = 7
                e.wait_ge(ve_ev, 32 * (q + 1))
                e.dma_start(out=xnT_sp[:, :, q * 512:(q + 1) * 512],
                            in_=stage[q % 2][:]).then_inc(sp1p[q % 2], 16)
                e.wait_ge(sp1p[0], 16 * 4)
                e.wait_ge(sp1p[1], 16 * 4)

            @block.gpsimd
            def _(e):
                e.dma_start(out=wo_bf[:, :], in_=wo[:, :], max_dma_last_dim=8192).then_inc(ld0, 16)
                for op in range(32):
                    e.dma_start(out=al_sb[:, op * 4:(op + 1) * 4],
                                in_=alphas_t[op * 128:(op + 1) * 128, :]).then_inc(ld0, 16)
                for k in range(4):
                    for hf in range(2):
                        e.dma_start(out=bases_bf[k, hf * 2048:(hf + 1) * 2048, :],
                                    in_=bases[k, hf * 2048:(hf + 1) * 2048, :],
                                    max_dma_last_dim=2048).then_inc(ld0, 16)

            @block.vector
            def _(e):
                e.memset(ones1[:], 1.0)
                e.memset(onesr[:], 1.0)
                def evac_quad(q):
                    if q >= 2:
                        e.wait_ge(sp1p[q % 2], 16 * (q // 2))
                    for dt in range(32):
                        gi = q * 32 + dt
                        e.wait_ge(te1, gi + 1)
                        cp = e.tensor_copy(stage[q % 2][:, dt * 512:(dt + 1) * 512],
                                           ps_t[dt % 2][:])
                        if dt < 31:
                            cp.then_inc(ve_ev, 1)
                    e.engine_nop()
                    e.engine_nop().then_inc(ve_ev, 1)

                e.memset(eps_sb[:], EPS)
                for pt in range(32):
                    p = pt % 2
                    e.wait_ge(ld1[pt % 2], 16 * (pt // 2 + 1))
                    e.reduce_sum(s1[p][:], x_sb[p][:], AX.X)
                    e.scalar_tensor_tensor(sq_scr[:], x_sb[p][:], 1.0, x_sb[p][:],
                                           ALU.mult, ALU.mult, accum_out=s2[p][:])
                    e.tensor_scalar_mul(mu[p][:], s1[p][:], 1.0 / D)
                    e.tensor_scalar_mul(nb[p][:], s1[p][:], -1.0 / D)
                    for _i in range(4):
                        e.engine_nop()
                    e.scalar_tensor_tensor(tp[p][:], mu[p][:], mu[p][:], eps_sb[:],
                                           ALU.mult, ALU.subtract)
                    for _i in range(4):
                        e.engine_nop()
                    e.scalar_tensor_tensor(rr[p][:], s2[p][:], 1.0 / D, tp[p][:],
                                           ALU.mult, ALU.subtract)
                    for _i in range(4):
                        e.engine_nop()
                    e.reciprocal(rr[p][:], rr[p][:])
                    for _i in range(3):
                        e.engine_nop()
                    e.engine_nop().then_inc(ve_st, 1)
                    if pt >= 4 and pt % 4 == 0:
                        evac_quad(pt // 4 - 1)
                evac_quad(7)

            @block.scalar
            def _(e):
                for pt in range(32):
                    p = pt % 2
                    e.wait_ge(ve_st, pt + 1)
                    e.activation(rr[p][:], rr[p][:], AF.Sqrt)
                    e.activation(nb[p][:], nb[p][:], AF.Identity, scale=rr[p][:])
                    if pt >= 4:
                        e.wait_ge(te1, 32 * (pt // 4))
                    e.activation(xn_bf[pt % 4][:], x_sb[p][:], AF.Identity,
                                 bias=nb[p][:], scale=rr[p][:])
                    e.copy(out=dum_sb[:], in_=dum_sb[:]).then_inc(se_nm, 1)

            @block.tensor
            def _(e):
                e.wait_ge(ldi, 16)
                for q in range(8):
                    e.wait_ge(se_nm, 4 * (q + 1))
                    for dt in range(32):
                        gi = q * 32 + dt
                        if gi >= 2:
                            e.wait_ge(ve_ev, gi - 1)
                        for j in range(4):
                            mm = e.transpose(ps_t[dt % 2][:, j * 128:(j + 1) * 128],
                                             xn_bf[j][:, dt * 128:(dt + 1) * 128], ident_sb[:])
                        mm.then_inc(te1, 1)

            barrier()

        with ExitStack() as ctx:
            E = ctx.enter_context
            wq_sb = [E(nc.sbuf_tensor(f"wq_sb{i}", [128, D], F32)) for i in range(2)]
            ta = [E(nc.sbuf_tensor(f"ta{i}", [128, 1024], F32)) for i in range(2)]
            tb = [E(nc.sbuf_tensor(f"tb{i}", [128, 1024], F32)) for i in range(2)]
            wq_eff = E(nc.sbuf_tensor("wq_eff", [128, NT * 1024], BF16))
            xc_sb = [E(nc.sbuf_tensor(f"xc_sb{i}", [128, NT * 256], BF16)) for i in range(2)]
            qstg = [E(nc.sbuf_tensor(f"qstg{i}", [128, 8 * 256], BF16)) for i in range(2)]
            ps_q = [E(nc.psum_tensor(f"ps_q{i}", [128, 256], F32)) for i in range(8)]

            ld_wq = [GE(nc.semaphore(f"ld_wq{i}")) for i in range(2)]
            ve_ab = GE(nc.semaphore("ve_ab"))
            ve_eff = GE(nc.semaphore("ve_eff"))
            ld_xc = [GE(nc.semaphore(f"ld_xc{i}")) for i in range(2)]
            te2a = GE(nc.semaphore("te2a"))
            se2a = GE(nc.semaphore("se2a"))
            sp2a = [GE(nc.semaphore(f"sp2a{i}")) for i in range(2)]

            @block.scalar
            def _(e):
                for it in range(32):
                    if it >= 2:
                        e.wait_ge(ve_ab, it - 1)
                    e.dma_start(out=wq_sb[it % 2][:], in_=wq[it * 128:(it + 1) * 128, :]).then_inc(ld_wq[it % 2], 16)
                for sc in range(8):
                    e.wait_ge(te2a, sc + 1)
                    if sc >= 2:
                        e.wait_ge(sp2a[sc % 2], 16 * (sc // 2))
                    for mt in range(8):
                        cp = e.copy(out=qstg[sc % 2][:, mt * 256:(mt + 1) * 256],
                                    in_=ps_q[mt][:])
                        if mt < 7:
                            cp.then_inc(se2a, 1)
                    e.copy(out=dum_sb[:], in_=dum_sb[:]).then_inc(se2a, 1)

            @block.vector
            def _(e):
                def hj(apx):
                    return apx.rearrange("p (h j) -> p h j", h=8, j=128)
                for it in range(32):
                    p = it % 2
                    e.wait_ge(ld_wq[it % 2], 16 * (it // 2 + 1))
                    wv_ = wq_sb[p][:].rearrange("p (h gj) -> p h gj", h=8, gj=512)
                    e.tensor_tensor(hj(ta[p][:]), wv_[:, :, 0:128], wv_[:, :, 128:256], ALU.add)
                    e.tensor_tensor(hj(tb[p][:]), wv_[:, :, 256:384], wv_[:, :, 384:512],
                                    ALU.add).then_inc(ve_ab, 1)
                    if it >= 1:
                        pp = (it - 1) % 2
                        e.tensor_tensor(hj(wq_eff[:, (it - 1) * 1024: it * 1024]),
                                        hj(ta[pp][:]), hj(tb[pp][:]), ALU.add)
                        e.engine_nop()
                        e.engine_nop().then_inc(ve_eff, 1)
                e.engine_nop()
                e.tensor_tensor(hj(wq_eff[:, 31 * 1024: 32 * 1024]),
                                hj(ta[1][:]), hj(tb[1][:]), ALU.add)
                e.engine_nop()
                e.engine_nop().then_inc(ve_eff, 1)

            @block.sync
            def _(e):
                def spill(sc):
                    e.wait_ge(se2a, 8 * (sc + 1))
                    e.dma_start(out=qT_sp[:, :, sc * 256:(sc + 1) * 256],
                                in_=qstg[sc % 2][:]).then_inc(sp2a[sc % 2], 16)
                for sc in range(8):
                    if sc >= 2:
                        e.wait_ge(te2a, sc - 1)
                    e.dma_start(out=xc_sb[sc % 2][:],
                                in_=xnT_sp[:, :, sc * 256:(sc + 1) * 256]).then_inc(ld_xc[sc % 2], 16)
                    if sc >= 2:
                        spill(sc - 2)
                spill(6)
                spill(7)
                e.wait_ge(sp2a[0], 16 * 4)
                e.wait_ge(sp2a[1], 16 * 4)

            @block.tensor
            def _(e):
                for sc in range(8):
                    e.wait_ge(ld_xc[sc % 2], 16 * (sc // 2 + 1))
                    if sc >= 1:
                        e.wait_ge(se2a, 8 * sc)
                    for it in range(32):
                        if sc == 0:
                            e.wait_ge(ve_eff, min(it + 2, 32))
                        for mt in range(8):
                            mm = e.matmul(ps_q[mt][:],
                                          wq_eff[:, it * 1024 + mt * 128: it * 1024 + (mt + 1) * 128],
                                          xc_sb[sc % 2][:, it * 256:(it + 1) * 256],
                                          start=(it == 0), stop=(it == 31))
                    mm.then_inc(te2a, 1)

            barrier()

        def proj_kv(which):
            with ExitStack() as ctx:
                E = ctx.enter_context
                w_sb = E(nc.sbuf_tensor(f"w_sb_{which}", [128, NT * 1024], BF16))
                c_sb = [E(nc.sbuf_tensor(f"c_sb_{which}{i}", [128, NT * 512], BF16)) for i in range(2)]
                kstg = [E(nc.sbuf_tensor(f"stg_{which}{i}", [128, 8 * 512], BF16)) for i in range(2)]
                ps = [E(nc.psum_tensor(f"ps_{which}{i}", [128, 512], F32)) for i in range(2)]

                ld_w = GE(nc.semaphore(f"ld_w{which}"))
                ld_c = [GE(nc.semaphore(f"ld_c{which}{i}")) for i in range(2)]
                te = GE(nc.semaphore(f"te2b{which}"))
                se = GE(nc.semaphore(f"se2b{which}"))
                sp = [GE(nc.semaphore(f"sp2b{which}{i}")) for i in range(2)]
                wsrc = wk if which == "k" else wv
                n_per_sc = 8 if which == "k" else 4

                @block.gpsimd
                def _(e):
                    for it in range(32):
                        e.dma_start(out=w_sb[:, it * 1024:(it + 1) * 1024],
                                    in_=wsrc[it * 128:(it + 1) * 128, :]).then_inc(ld_w, 16)

                @block.sync
                def _(e):
                    def spill_k(sc):
                        e.wait_ge(se, 8 * (sc + 1))
                        e.dma_start(out=kT_sp[:, :, sc * 512:(sc + 1) * 512],
                                    in_=kstg[sc % 2][:]).then_inc(sp[sc % 2], 16)

                    def spill_v(gi):
                        e.wait_ge(se, 2 * (gi + 1))
                        e.dma_start(out=v_sp[:, gi, :],
                                    in_=kstg[gi % 2][:, 0:1024]).then_inc(sp[gi % 2], 16)

                    for sc in range(8):
                        if sc >= 2:
                            if which == "k":
                                spill_k(sc - 2)
                            else:
                                for gi in range((sc - 2) * 4, (sc - 1) * 4):
                                    spill_v(gi)
                            e.wait_ge(te, (sc - 1) * n_per_sc)
                        e.dma_start(out=c_sb[sc % 2][:],
                                    in_=xnT_sp[:, :, sc * 512:(sc + 1) * 512]).then_inc(ld_c[sc % 2], 16)
                    if which == "k":
                        spill_k(6)
                        spill_k(7)
                        e.wait_ge(sp[0], 16 * 4)
                        e.wait_ge(sp[1], 16 * 4)
                    else:
                        for gi in range(24, 32):
                            spill_v(gi)
                        e.wait_ge(sp[0], 16 * 16)
                        e.wait_ge(sp[1], 16 * 16)

                if which == "k":
                    @block.tensor
                    def _(e):
                        e.wait_ge(ld_w, 16 * 32)
                        for sc in range(8):
                            e.wait_ge(ld_c[sc % 2], 16 * (sc // 2 + 1))
                            for mt in range(8):
                                gi = sc * 8 + mt
                                if gi >= 2:
                                    e.wait_ge(se, gi - 1)
                                for it in range(32):
                                    mm = e.matmul(ps[mt % 2][:],
                                                  w_sb[:, it * 1024 + mt * 128: it * 1024 + (mt + 1) * 128],
                                                  c_sb[sc % 2][:, it * 512:(it + 1) * 512],
                                                  start=(it == 0), stop=(it == 31))
                                mm.then_inc(te, 1)

                    @block.scalar
                    def _(e):
                        for sc in range(8):
                            if sc >= 2:
                                e.wait_ge(sp[sc % 2], 16 * (sc // 2))
                            for mt in range(8):
                                gi = sc * 8 + mt
                                e.wait_ge(te, gi + 1)
                                cp = e.copy(out=kstg[sc % 2][:, mt * 512:(mt + 1) * 512],
                                            in_=ps[mt % 2][:])
                                if mt < 7:
                                    cp.then_inc(se, 1)
                            e.copy(out=dum_sb[:], in_=dum_sb[:]).then_inc(se, 1)
                else:
                    @block.tensor
                    def _(e):
                        e.wait_ge(ld_w, 16 * 32)
                        for sc in range(8):
                            e.wait_ge(ld_c[sc % 2], 16 * (sc // 2 + 1))
                            for smt in range(4):
                                gi = sc * 4 + smt
                                if gi >= 1:
                                    e.wait_ge(se, 2 * gi)
                                for nsl in range(2):
                                    for it in range(32):
                                        mm = e.matmul(ps[nsl][:],
                                                      c_sb[sc % 2][:, it * 512 + smt * 128: it * 512 + (smt + 1) * 128],
                                                      w_sb[:, it * 1024 + nsl * 512: it * 1024 + (nsl + 1) * 512],
                                                      start=(it == 0), stop=(it == 31))
                                mm.then_inc(te, 1)

                    @block.scalar
                    def _(e):
                        for sc in range(8):
                            for smt in range(4):
                                gi = sc * 4 + smt
                                e.wait_ge(te, gi + 1)
                                if gi >= 2:
                                    e.wait_ge(sp[gi % 2], 16 * (gi // 2))
                                e.copy(out=kstg[smt % 2][:, 0:512], in_=ps[0][:]).then_inc(se, 1)
                                e.copy(out=kstg[smt % 2][:, 512:1024], in_=ps[1][:])
                                e.copy(out=dum_sb[:], in_=dum_sb[:]).then_inc(se, 1)

                barrier()

        proj_kv("k")
        proj_kv("v")

        with ExitStack() as ctx:
            E = ctx.enter_context
            b_sb = [E(nc.sbuf_tensor(f"b_sb{i}", [128, D], BF16)) for i in range(4)]
            wq_ring = [E(nc.sbuf_tensor(f"wq_ring{i}", [128, D], BF16)) for i in range(8)]
            wstg = [E(nc.sbuf_tensor(f"wstg{i}", [128, 16 * 512], BF16)) for i in range(2)]
            ps_t2 = [E(nc.psum_tensor(f"ps_t2{i}", [128, 512], BF16)) for i in range(2)]

            ld_b = [GE(nc.semaphore(f"ld_b{i}")) for i in range(4)]
            ve_b = GE(nc.semaphore("ve_b"))
            ve_acc = GE(nc.semaphore("ve_acc"))
            te2c = GE(nc.semaphore("te2c"))
            ve_ev2 = GE(nc.semaphore("ve_ev2"))
            sp2c = [GE(nc.semaphore(f"sp2c{i}")) for i in range(2)]

            @block.sync
            def _(e):
                e.wait_ge(ld0, 16 * 41)
                for op in range(32):
                    for k in range(4):
                        gi = op * 4 + k
                        if gi >= 4:
                            e.wait_ge(ve_b, gi - 3)
                        e.dma_start(out=b_sb[gi % 4][:],
                                    in_=bases_bf[k, op * 128:(op + 1) * 128, :]).then_inc(ld_b[gi % 4], 16)

            @block.vector
            def _(e):
                def emit_build(op):
                    r = wq_ring[op % 8]
                    og = op // 4
                    if og >= 2:
                        e.wait_ge(te2c, (og - 1) * 32)
                    for k in range(4):
                        gi = op * 4 + k
                        e.wait_ge(ld_b[gi % 4], 16 * (gi // 4 + 1))
                        if k == 0:
                            e.tensor_scalar(r[:], b_sb[gi % 4][:],
                                            al_sb[:, op * 4: op * 4 + 1], None,
                                            ALU.mult).then_inc(ve_b, 1)
                        else:
                            e.scalar_tensor_tensor(r[:], b_sb[gi % 4][:],
                                                   al_sb[:, op * 4 + k: op * 4 + k + 1], r[:],
                                                   ALU.mult, ALU.add).then_inc(ve_b, 1)
                        e.engine_nop()
                    e.engine_nop().then_inc(ve_acc, 1)

                def emit_evac_unit(u):
                    og, hf = u // 2, u % 2
                    if u >= 2:
                        e.wait_ge(sp2c[u % 2], 16 * (u // 2))
                    for itl in range(16):
                        it = hf * 16 + itl
                        gi = og * 32 + it
                        e.wait_ge(te2c, gi + 1)
                        cp = e.tensor_copy(wstg[u % 2][:, itl * 512:(itl + 1) * 512],
                                           ps_t2[it % 2][:])
                        if itl < 15:
                            cp.then_inc(ve_ev2, 1)
                    e.engine_nop()
                    e.engine_nop().then_inc(ve_ev2, 1)

                e.wait_ge(ld0, 16 * 33)
                for op in range(8):
                    emit_build(op)
                for og in range(2, 8):
                    emit_evac_unit((og - 2) * 2)
                    emit_evac_unit((og - 2) * 2 + 1)
                    for j in range(4):
                        emit_build(og * 4 + j)
                for u in range(12, 16):
                    emit_evac_unit(u)

            @block.tensor
            def _(e):
                for og in range(8):
                    e.wait_ge(ve_acc, 4 * (og + 1))
                    for it in range(32):
                        gi = og * 32 + it
                        if gi >= 2:
                            e.wait_ge(ve_ev2, gi - 1)
                        for j in range(4):
                            mm = e.transpose(ps_t2[it % 2][:, j * 128:(j + 1) * 128],
                                             wq_ring[(og * 4 + j) % 8][:, it * 128:(it + 1) * 128],
                                             ident_sb[:])
                        mm.then_inc(te2c, 1)

            @block.scalar
            def _(e):
                for u in range(16):
                    og, hf = u // 2, u % 2
                    e.wait_ge(ve_ev2, 16 * (u + 1))
                    e.dma_start(out=wqT_sp[og, :, hf * 16:(hf + 1) * 16, :],
                                in_=wstg[u % 2][:]).then_inc(sp2c[u % 2], 16)
                e.wait_ge(sp2c[0], 16 * 8)
                e.wait_ge(sp2c[1], 16 * 8)

            barrier()

        with ExitStack() as ctx:
            E = ctx.enter_context
            kt_sb = E(nc.sbuf_tensor("kt_sb", [128, 4 * S], BF16))
            vg_sb = E(nc.sbuf_tensor("vg_sb", [128, NT * 512], BF16))
            qt_sb = E(nc.sbuf_tensor("qt_sb", [128, 4 * OWN], BF16))
            expT = [E(nc.sbuf_tensor(f"expT{i}", [128, 1024], BF16)) for i in range(2)]
            rec_sb = E(nc.sbuf_tensor("rec_sb", [1, 512], F32))
            bc_sb = E(nc.sbuf_tensor("bc_sb", [128, 512], F32))
            ps_s2 = [E(nc.psum_tensor(f"ps_s2{i}", [128, 1024], F32)) for i in range(2)]
            ps_pv = [E(nc.psum_tensor(f"ps_pv{i}", [128, 512], F32)) for i in range(2)]
            ps_dn = [E(nc.psum_tensor(f"ps_dn{i}", [1, 512], F32)) for i in range(2)]

            ld3 = GE(nc.semaphore("ld3"))
            te_s = GE(nc.semaphore("te_s"))
            se_e = GE(nc.semaphore("se_e"))
            te_pvk = GE(nc.semaphore("te_pvk"))
            ve_r = GE(nc.semaphore("ve_r"))
            te_bc = GE(nc.semaphore("te_bc"))
            se_bc = GE(nc.semaphore("se_bc"))
            ve_av = GE(nc.semaphore("ve_av"))

            @block.sync
            def _(e):
                for hg in range(2):
                    if hg == 1:
                        e.wait_ge(te_bc, 16)
                    for hh in range(4):
                        e.dma_start(out=kt_sb[:, hh * S:(hh + 1) * S],
                                    in_=kT_sp[:, hg * 4 + hh, :]).then_inc(ld3, 16)
                        e.dma_start(out=qt_sb[:, hh * OWN:(hh + 1) * OWN],
                                    in_=qT_sp[:, hg * 4 + hh, :]).then_inc(ld3, 16)
                    e.dma_start(out=vg_sb[:], in_=v_sp[:, :, hg * 512:(hg + 1) * 512]).then_inc(ld3, 16)

            @block.tensor
            def _(e):
                for hg in range(2):
                    e.wait_ge(ld3, 16 * 9 * (hg + 1))
                    for hh in range(4):
                        for tc in range(4):
                            m = (hg * 4 + hh) * 4 + tc
                            if m >= 2:
                                e.wait_ge(ve_av, m - 1)
                            if m >= 1:
                                e.wait_ge(se_bc, m)
                            for kp in range(16):
                                g = m * 16 + kp
                                if g >= 2:
                                    e.wait_ge(se_e, g - 1)
                                for j in range(2):
                                    kt = 2 * kp + j
                                    mm = e.matmul(ps_s2[kp % 2][:, j * 512:(j + 1) * 512],
                                                  kt_sb[:, hh * S + kt * 128: hh * S + (kt + 1) * 128],
                                                  qt_sb[:, hh * OWN + tc * 512: hh * OWN + (tc + 1) * 512],
                                                  start=True, stop=True)
                                mm.then_inc(te_s, 1)
                                if kp >= 1:
                                    kq = kp - 1
                                    e.wait_ge(se_e, m * 16 + kq + 1)
                                    for j in range(2):
                                        kt = 2 * kq + j
                                        e.matmul(ps_pv[m % 2][:],
                                                 vg_sb[:, kt * 512 + hh * 128: kt * 512 + (hh + 1) * 128],
                                                 expT[kq % 2][:, j * 512:(j + 1) * 512],
                                                 start=(kt == 0), stop=(kt == 31))
                                        mm = e.matmul(ps_dn[m % 2][:], ones1[:],
                                                      expT[kq % 2][:, j * 512:(j + 1) * 512],
                                                      start=(kt == 0), stop=(kt == 31))
                                    mm.then_inc(te_pvk, 1)
                            kq = 15
                            e.wait_ge(se_e, m * 16 + kq + 1)
                            for j in range(2):
                                kt = 2 * kq + j
                                e.matmul(ps_pv[m % 2][:],
                                         vg_sb[:, kt * 512 + hh * 128: kt * 512 + (hh + 1) * 128],
                                         expT[kq % 2][:, j * 512:(j + 1) * 512],
                                         start=(kt == 0), stop=(kt == 31))
                                mm = e.matmul(ps_dn[m % 2][:], ones1[:],
                                              expT[kq % 2][:, j * 512:(j + 1) * 512],
                                              start=(kt == 0), stop=(kt == 31))
                            mm.then_inc(te_pvk, 1)
                            e.wait_ge(ve_r, m + 1)
                            e.matmul(ps_s2[0][:, 0:512], onesr[:], rec_sb[:],
                                     start=True, stop=True).then_inc(te_bc, 1)

            @block.scalar
            def _(e):
                for m in range(32):
                    for kp in range(16):
                        g = m * 16 + kp
                        e.wait_ge(te_s, g + 1)
                        if g >= 2:
                            e.wait_ge(te_pvk, g - 1)
                        e.activation(expT[kp % 2][:], ps_s2[kp % 2][:], AF.Exp,
                                     scale=SC)
                        e.copy(out=dum_sb[:], in_=dum_sb[:]).then_inc(se_e, 1)
                    e.wait_ge(te_bc, m + 1)
                    if m >= 1:
                        e.wait_ge(ve_av, m)
                    e.copy(out=bc_sb[:], in_=ps_s2[0][:, 0:512])
                    e.copy(out=dum_sb[:], in_=dum_sb[:]).then_inc(se_bc, 1)

            @block.vector
            def _(e):
                for m in range(32):
                    hd = m // 4
                    tc = m % 4
                    e.wait_ge(te_pvk, 16 * (m + 1))
                    if m >= 1:
                        e.wait_ge(te_bc, m)
                    e.reciprocal(rec_sb[:], ps_dn[m % 2][0:1, :])
                    e.engine_nop()
                    e.engine_nop().then_inc(ve_r, 1)
                    e.wait_ge(se_bc, m + 1)
                    e.scalar_tensor_tensor(av_sb[:, hd * OWN + tc * 512: hd * OWN + (tc + 1) * 512],
                                           ps_pv[m % 2][:], 1.0, bc_sb[:],
                                           ALU.mult, ALU.mult).then_inc(ve_av, 1)

            barrier()

        with ExitStack() as ctx:
            E = ctx.enter_context
            xh_sb = E(nc.sbuf_tensor("xh_sb", [128, NT * 1024], BF16))
            wqT_sb = [E(nc.sbuf_tensor(f"wqT_sb{i}", [128, NT * 512], BF16)) for i in range(2)]
            g_st = [E(nc.sbuf_tensor(f"g_st{i}", [128, 512], BF16)) for i in range(2)]
            ps_f = [E(nc.psum_tensor(f"ps_f{i}", [128, 512], F32)) for i in range(2)]

            ld4x = GE(nc.semaphore("ld4x"))
            ld4w = [GE(nc.semaphore(f"ld4w{i}")) for i in range(2)]
            te4 = GE(nc.semaphore("te4"))
            se4 = GE(nc.semaphore("se4"))
            sp4 = [GE(nc.semaphore(f"sp4_{i}")) for i in range(2)]

            @block.sync
            def _(e):
                def spill_g(gj):
                    rh2, oc2, mt2 = gj // 64, (gj // 8) % 8, gj % 8
                    e.wait_ge(se4, gj + 1)
                    e.dma_start(out=g_sp[rh2 * 1024 + mt2 * 128: rh2 * 1024 + (mt2 + 1) * 128,
                                         oc2 * 512:(oc2 + 1) * 512],
                                in_=g_st[gj % 2][:]).then_inc(sp4[gj % 2], 16)
                e.dma_start(out=xh_sb[:], in_=xnT_sp[:, :, 0:1024]).then_inc(ld4x, 16)
                for gj in range(64):
                    spill_g(gj)
                e.wait_ge(te4, 64)
                e.dma_start(out=xh_sb[:], in_=xnT_sp[:, :, 1024:2048]).then_inc(ld4x, 16)
                for gj in range(64, 128):
                    spill_g(gj)
                e.wait_ge(sp4[0], 16 * 64)
                e.wait_ge(sp4[1], 16 * 64)

            @block.gpsimd
            def _(e):
                for rh in range(2):
                    for oc in range(8):
                        gi = rh * 8 + oc
                        if gi >= 2:
                            e.wait_ge(te4, (gi - 1) * 8)
                        e.dma_start(out=wqT_sb[gi % 2][:], in_=wqT_sp[oc, :, :, :]).then_inc(ld4w[gi % 2], 16)

            @block.tensor
            def _(e):
                for rh in range(2):
                    e.wait_ge(ld4x, 16 * (rh + 1))
                    for oc in range(8):
                        gi = rh * 8 + oc
                        e.wait_ge(ld4w[gi % 2], 16 * (gi // 2 + 1))
                        for mt in range(8):
                            gj = gi * 8 + mt
                            if gj >= 2:
                                e.wait_ge(se4, gj - 1)
                            for it in range(32):
                                mm = e.matmul(ps_f[mt % 2][:],
                                              xh_sb[:, it * 1024 + mt * 128: it * 1024 + (mt + 1) * 128],
                                              wqT_sb[gi % 2][:, it * 512:(it + 1) * 512],
                                              start=(it == 0), stop=(it == 31))
                            mm.then_inc(te4, 1)

            @block.scalar
            def _(e):
                for rh in range(2):
                    for oc in range(8):
                        for mt in range(8):
                            gj = (rh * 8 + oc) * 8 + mt
                            e.wait_ge(te4, gj + 1)
                            if gj >= 2:
                                e.wait_ge(sp4[gj % 2], 16 * (gj // 2))
                            e.activation(g_st[gj % 2][:], ps_f[mt % 2][:], AF.Silu)
                            e.copy(out=dum_sb[:], in_=dum_sb[:]).then_inc(se4, 1)

            barrier()

        with ExitStack() as ctx:
            E = ctx.enter_context
            wo_sb = E(nc.sbuf_tensor("wo_sb", [128, 8 * D], BF16))
            g_in = [E(nc.sbuf_tensor(f"g_in{i}", [128, 512], BF16)) for i in range(2)]
            x_in = [E(nc.sbuf_tensor(f"x_in{i}", [128, 512], F32)) for i in range(2)]
            t_sb = [E(nc.sbuf_tensor(f"t_sb{i}", [128, 512], F32)) for i in range(2)]
            o_st = [E(nc.sbuf_tensor(f"o_st{i}", [128, 512], F32)) for i in range(2)]
            ps_o = [E(nc.psum_tensor(f"ps_o{i}", [128, 512], F32)) for i in range(2)]

            ld5w = GE(nc.semaphore("ld5w"))
            ldg = [GE(nc.semaphore(f"ldg{i}")) for i in range(2)]
            ldx = [GE(nc.semaphore(f"ldx{i}")) for i in range(2)]
            te5 = GE(nc.semaphore("te5"))
            ve5s = GE(nc.semaphore("ve5s"))
            ve5t = GE(nc.semaphore("ve5t"))
            sp5 = [GE(nc.semaphore(f"sp5_{i}")) for i in range(2)]

            @block.gpsimd
            def _(e):
                e.wait_ge(ld0, 16 * 41)
                for kt in range(8):
                    e.dma_start(out=wo_sb[:, kt * D:(kt + 1) * D],
                                in_=wo_bf[kt * 128:(kt + 1) * 128, :]).then_inc(ld5w, 16)

            @block.sync
            def _(e):
                for i in range(128):
                    mt, dc = i // 8, i % 8
                    if i >= 2:
                        e.wait_ge(ve5s, i - 1)
                        e.wait_ge(ve5t, i - 1)
                    e.dma_start(out=g_in[i % 2][:],
                                in_=g_sp[mt * 128:(mt + 1) * 128, dc * 512:(dc + 1) * 512]).then_inc(ldg[i % 2], 16)
                    e.dma_start(out=x_in[i % 2][:],
                                in_=x[mt * 128:(mt + 1) * 128, dc * 512:(dc + 1) * 512]).then_inc(ldx[i % 2], 16)

            @block.tensor
            def _(e):
                e.wait_ge(ld5w, 16 * 8)
                for i in range(128):
                    mt, dc = i // 8, i % 8
                    if i >= 2:
                        e.wait_ge(ve5s, i - 1)
                    for kt in range(8):
                        mm = e.matmul(ps_o[i % 2][:],
                                      av_sb[:, kt * OWN + mt * 128: kt * OWN + (mt + 1) * 128],
                                      wo_sb[:, kt * D + dc * 512: kt * D + (dc + 1) * 512],
                                      start=(kt == 0), stop=(kt == 7))
                    mm.then_inc(te5, 1)

            @block.vector
            def _(e):
                for i in range(128):
                    e.wait_ge(te5, i + 1)
                    e.wait_ge(ldg[i % 2], 16 * (i // 2 + 1))
                    e.scalar_tensor_tensor(t_sb[i % 2][:], ps_o[i % 2][:], 1.0, g_in[i % 2][:],
                                           ALU.mult, ALU.mult).then_inc(ve5s, 1)
                    if i >= 1:
                        j = i - 1
                        e.wait_ge(ldx[j % 2], 16 * (j // 2 + 1))
                        if j >= 2:
                            e.wait_ge(sp5[j % 2], 16 * (j // 2))
                        e.tensor_tensor(o_st[j % 2][:], t_sb[j % 2][:], x_in[j % 2][:],
                                        ALU.add)
                        e.engine_nop().then_inc(ve5t, 1)
                j = 127
                e.engine_nop()
                e.wait_ge(ldx[j % 2], 16 * (j // 2 + 1))
                e.wait_ge(sp5[j % 2], 16 * (j // 2))
                e.tensor_tensor(o_st[j % 2][:], t_sb[j % 2][:], x_in[j % 2][:],
                                ALU.add)
                e.engine_nop().then_inc(ve5t, 1)

            @block.scalar
            def _(e):
                for i in range(128):
                    mt, dc = i // 8, i % 8
                    e.wait_ge(ve5t, i + 1)
                    e.dma_start(out=out[mt * 128:(mt + 1) * 128, dc * 512:(dc + 1) * 512],
                                in_=o_st[i % 2][:]).then_inc(sp5[i % 2], 16)
                e.wait_ge(sp5[0], 16 * 64)
                e.wait_ge(sp5[1], 16 * 64)

    return nc


_NC_CACHE = None


def kernel(L0_attn_x, L0_attn_wq, L0_attn_wk, L0_attn_wv, L0_attn_wo, L0_ffn_bases, L0_ffn_alphas):
    global _NC_CACHE
    x = np.ascontiguousarray(np.asarray(L0_attn_x, dtype=np.float32))
    wq = np.ascontiguousarray(np.asarray(L0_attn_wq, dtype=np.float32))
    wk = np.ascontiguousarray(np.asarray(L0_attn_wk, dtype=np.float32))
    wv = np.ascontiguousarray(np.asarray(L0_attn_wv, dtype=np.float32))
    wo = np.ascontiguousarray(np.asarray(L0_attn_wo, dtype=np.float32))
    bases = np.ascontiguousarray(np.asarray(L0_ffn_bases, dtype=np.int8))
    alphas_t = np.ascontiguousarray(np.asarray(L0_ffn_alphas, dtype=np.float32).reshape(4, D).T)
    ident = np.eye(128, dtype=ml_dtypes.bfloat16)

    if _NC_CACHE is None:
        _NC_CACHE = build_nc()
    nc = _NC_CACHE

    in_maps = []
    for c in range(8):
        b, h = c // 2, c % 2
        xb = x[b]
        xp = np.ascontiguousarray(
            np.concatenate([xb[h * OWN:(h + 1) * OWN], xb[(1 - h) * OWN:(2 - h) * OWN]], axis=0))
        in_maps.append({"x": xp, "wq": wq, "wk": wk, "wv": wv, "wo": wo,
                        "bases": bases, "alphas_t": alphas_t, "ident": ident})

    res = run_bass_kernel_spmd(nc, in_maps, core_ids=list(range(8)))

    outa = np.empty((4, S, D), np.float32)
    for c in range(8):
        b, h = c // 2, c % 2
        outa[b, h * OWN:(h + 1) * OWN, :] = res.results[c]["out"]
    return outa


# version 10
# speedup vs baseline: 1.0310x; 1.0137x over previous
import numpy as np
import ml_dtypes
import concourse.bass as bass
import concourse.mybir as mybir
from concourse.bass_utils import run_bass_kernel_spmd
from contextlib import ExitStack

F32 = mybir.dt.float32
BF16 = mybir.dt.bfloat16
I8 = mybir.dt.int8
AF = mybir.ActivationFunctionType
ALU = mybir.AluOpType
AX = mybir.AxisListType

D = 4096
S = 4096
OWN = 2048
NT = D // 128
HEADS = 8
KV = 1024
SC = 0.08838834764831845
EPS = 1e-5


def build_nc():
    nc = bass.Bass("TRN2", target_bir_lowering=False)

    x = nc.dram_tensor("x", [S, D], F32, kind="ExternalInput").ap()
    wq = nc.dram_tensor("wq", [D, D], F32, kind="ExternalInput").ap()
    wk = nc.dram_tensor("wk", [D, KV], F32, kind="ExternalInput").ap()
    wv = nc.dram_tensor("wv", [D, KV], F32, kind="ExternalInput").ap()
    wo = nc.dram_tensor("wo", [KV, D], F32, kind="ExternalInput").ap()
    bases = nc.dram_tensor("bases", [4, D, D], I8, kind="ExternalInput").ap()
    alphas_t = nc.dram_tensor("alphas_t", [D, 4], F32, kind="ExternalInput").ap()
    ident_in = nc.dram_tensor("ident", [128, 128], BF16, kind="ExternalInput").ap()
    out = nc.dram_tensor("out", [OWN, D], F32, kind="ExternalOutput").ap()

    xnT_sp = nc.dram_tensor("xnT_sp", [128, NT, S], BF16, kind="Internal").ap()
    qT_sp = nc.dram_tensor("qT_sp", [128, HEADS, OWN], BF16, kind="Internal").ap()
    kT_sp = nc.dram_tensor("kT_sp", [128, HEADS, S], BF16, kind="Internal").ap()
    v_sp = nc.dram_tensor("v_sp", [128, NT, KV], BF16, kind="Internal").ap()
    wqT_sp = nc.dram_tensor("wqT_sp", [8, 128, NT, 512], BF16, kind="Internal").ap()
    g_sp = nc.dram_tensor("g_sp", [OWN, D], BF16, kind="Internal").ap()
    wo_bf = nc.dram_tensor("wo_bf", [KV, D], BF16, kind="Internal").ap()
    bases_bf = nc.dram_tensor("bases_bf", [4, D, D], BF16, kind="Internal").ap()

    with ExitStack() as gctx:
        GE = gctx.enter_context
        ident_sb = GE(nc.sbuf_tensor("ident_sb", [128, 128], BF16))
        ones1 = GE(nc.sbuf_tensor("ones1", [128, 1], BF16))
        onesr = GE(nc.sbuf_tensor("onesr", [1, 128], F32))
        al_sb = GE(nc.sbuf_tensor("al_sb", [128, 128], F32))
        dum_sb = GE(nc.sbuf_tensor("dum_sb", [128, 1], F32))
        av_sb = GE(nc.sbuf_tensor("av_sb", [128, HEADS * OWN], BF16))

        bar = GE(nc.semaphore("bar"))
        nbar = [0]

        block = GE(nc.Block())

        def barrier():
            nbar[0] += 5
            tgt = nbar[0]
            for dec in (block.sync, block.scalar, block.vector, block.tensor, block.gpsimd):
                @dec
                def _(e):
                    e.sem_inc(bar, 1)
                    e.wait_ge(bar, tgt)

        with ExitStack() as ctx:
            E = ctx.enter_context
            x_sb = [E(nc.sbuf_tensor(f"x_sb{i}", [128, D], F32)) for i in range(2)]
            sq_scr = E(nc.sbuf_tensor("sq_scr", [128, D], F32))
            s1 = [E(nc.sbuf_tensor(f"s1_{i}", [128, 1], F32)) for i in range(2)]
            s2 = [E(nc.sbuf_tensor(f"s2_{i}", [128, 1], F32)) for i in range(2)]
            mu = [E(nc.sbuf_tensor(f"mu_{i}", [128, 1], F32)) for i in range(2)]
            rr = [E(nc.sbuf_tensor(f"rr_{i}", [128, 1], F32)) for i in range(2)]
            nb = [E(nc.sbuf_tensor(f"nb_{i}", [128, 1], F32)) for i in range(2)]
            tp = [E(nc.sbuf_tensor(f"tp_{i}", [128, 1], F32)) for i in range(2)]
            eps_sb = E(nc.sbuf_tensor("eps_sb", [128, 1], F32))
            xn_bf = [E(nc.sbuf_tensor(f"xn_bf{i}", [128, D], BF16)) for i in range(4)]
            stage = [E(nc.sbuf_tensor(f"stage{i}", [128, NT * 512], BF16)) for i in range(2)]
            ps_t = [E(nc.psum_tensor(f"ps_t{i}", [128, 512], BF16)) for i in range(2)]

            ldi = GE(nc.semaphore("ldi"))
            ld0 = GE(nc.semaphore("ld0"))
            ld1 = [GE(nc.semaphore(f"ld1_{i}")) for i in range(2)]
            sp1p = [GE(nc.semaphore(f"sp1_{i}")) for i in range(2)]
            se_sq = GE(nc.semaphore("se_sq"))
            ve_st = GE(nc.semaphore("ve_st"))
            se_r = GE(nc.semaphore("se_r"))
            ve_nb = GE(nc.semaphore("ve_nb"))
            se_nm = GE(nc.semaphore("se_nm"))
            te1 = GE(nc.semaphore("te1"))
            ve_ev = GE(nc.semaphore("ve_ev"))
            sp1 = GE(nc.semaphore("sp1"))

            @block.sync
            def _(e):
                e.dma_start(out=ident_sb[:], in_=ident_in[:, :]).then_inc(ldi, 16)
                for pt in range(32):
                    if pt >= 2:
                        e.wait_ge(se_nm, pt - 1)
                    e.dma_start(out=x_sb[pt % 2][:], in_=x[pt * 128:(pt + 1) * 128, :]).then_inc(ld1[pt % 2], 16)
                    if pt >= 7 and (pt - 7) % 4 == 0:
                        q = (pt - 7) // 4
                        e.wait_ge(ve_ev, 32 * (q + 1))
                        e.dma_start(out=xnT_sp[:, :, q * 512:(q + 1) * 512],
                                    in_=stage[q % 2][:]).then_inc(sp1p[q % 2], 16)
                q = 7
                e.wait_ge(ve_ev, 32 * (q + 1))
                e.dma_start(out=xnT_sp[:, :, q * 512:(q + 1) * 512],
                            in_=stage[q % 2][:]).then_inc(sp1p[q % 2], 16)
                e.wait_ge(sp1p[0], 16 * 4)
                e.wait_ge(sp1p[1], 16 * 4)

            @block.gpsimd
            def _(e):
                e.dma_start(out=wo_bf[:, :], in_=wo[:, :], max_dma_last_dim=8192).then_inc(ld0, 16)
                for op in range(32):
                    e.dma_start(out=al_sb[:, op * 4:(op + 1) * 4],
                                in_=alphas_t[op * 128:(op + 1) * 128, :]).then_inc(ld0, 16)


            @block.vector
            def _(e):
                e.memset(ones1[:], 1.0)
                e.memset(onesr[:], 1.0)
                def evac_quad(q):
                    if q >= 2:
                        e.wait_ge(sp1p[q % 2], 16 * (q // 2))
                    for dt in range(32):
                        gi = q * 32 + dt
                        e.wait_ge(te1, gi + 1)
                        cp = e.tensor_copy(stage[q % 2][:, dt * 512:(dt + 1) * 512],
                                           ps_t[dt % 2][:])
                        if dt < 31:
                            cp.then_inc(ve_ev, 1)
                    e.engine_nop()
                    e.engine_nop().then_inc(ve_ev, 1)

                e.memset(eps_sb[:], EPS)
                for pt in range(32):
                    p = pt % 2
                    e.wait_ge(ld1[pt % 2], 16 * (pt // 2 + 1))
                    e.reduce_sum(s1[p][:], x_sb[p][:], AX.X)
                    e.scalar_tensor_tensor(sq_scr[:], x_sb[p][:], 1.0, x_sb[p][:],
                                           ALU.mult, ALU.mult, accum_out=s2[p][:])
                    e.tensor_scalar_mul(mu[p][:], s1[p][:], 1.0 / D)
                    e.tensor_scalar_mul(nb[p][:], s1[p][:], -1.0 / D)
                    for _i in range(4):
                        e.engine_nop()
                    e.scalar_tensor_tensor(tp[p][:], mu[p][:], mu[p][:], eps_sb[:],
                                           ALU.mult, ALU.subtract)
                    for _i in range(4):
                        e.engine_nop()
                    e.scalar_tensor_tensor(rr[p][:], s2[p][:], 1.0 / D, tp[p][:],
                                           ALU.mult, ALU.subtract)
                    for _i in range(4):
                        e.engine_nop()
                    e.reciprocal(rr[p][:], rr[p][:])
                    for _i in range(3):
                        e.engine_nop()
                    e.engine_nop().then_inc(ve_st, 1)
                    if pt >= 4 and pt % 4 == 0:
                        evac_quad(pt // 4 - 1)
                evac_quad(7)

            @block.scalar
            def _(e):
                for pt in range(32):
                    p = pt % 2
                    e.wait_ge(ve_st, pt + 1)
                    e.activation(rr[p][:], rr[p][:], AF.Sqrt)
                    e.activation(nb[p][:], nb[p][:], AF.Identity, scale=rr[p][:])
                    if pt >= 4:
                        e.wait_ge(te1, 32 * (pt // 4))
                    e.activation(xn_bf[pt % 4][:], x_sb[p][:], AF.Identity,
                                 bias=nb[p][:], scale=rr[p][:])
                    e.copy(out=dum_sb[:], in_=dum_sb[:]).then_inc(se_nm, 1)

            @block.tensor
            def _(e):
                e.wait_ge(ldi, 16)
                for q in range(8):
                    e.wait_ge(se_nm, 4 * (q + 1))
                    for dt in range(32):
                        gi = q * 32 + dt
                        if gi >= 2:
                            e.wait_ge(ve_ev, gi - 1)
                        for j in range(4):
                            mm = e.transpose(ps_t[dt % 2][:, j * 128:(j + 1) * 128],
                                             xn_bf[j][:, dt * 128:(dt + 1) * 128], ident_sb[:])
                        mm.then_inc(te1, 1)

            barrier()

        with ExitStack() as ctx:
            E = ctx.enter_context
            wq_sb = [E(nc.sbuf_tensor(f"wq_sb{i}", [128, D], F32)) for i in range(2)]
            ta = [E(nc.sbuf_tensor(f"ta{i}", [128, 1024], F32)) for i in range(2)]
            tb = [E(nc.sbuf_tensor(f"tb{i}", [128, 1024], F32)) for i in range(2)]
            wq_eff = E(nc.sbuf_tensor("wq_eff", [128, NT * 1024], BF16))
            xc_sb = [E(nc.sbuf_tensor(f"xc_sb{i}", [128, NT * 256], BF16)) for i in range(2)]
            qstg = [E(nc.sbuf_tensor(f"qstg{i}", [128, 8 * 256], BF16)) for i in range(2)]
            ps_q = [E(nc.psum_tensor(f"ps_q{i}", [128, 256], F32)) for i in range(8)]

            ld_wq = [GE(nc.semaphore(f"ld_wq{i}")) for i in range(2)]
            ve_ab = GE(nc.semaphore("ve_ab"))
            ve_eff = GE(nc.semaphore("ve_eff"))
            ld_xc = [GE(nc.semaphore(f"ld_xc{i}")) for i in range(2)]
            te2a = GE(nc.semaphore("te2a"))
            se2a = GE(nc.semaphore("se2a"))
            sp2a = [GE(nc.semaphore(f"sp2a{i}")) for i in range(2)]

            @block.scalar
            def _(e):
                for it in range(32):
                    if it >= 2:
                        e.wait_ge(ve_ab, it - 1)
                    e.dma_start(out=wq_sb[it % 2][:], in_=wq[it * 128:(it + 1) * 128, :]).then_inc(ld_wq[it % 2], 16)
                for sc in range(8):
                    e.wait_ge(te2a, sc + 1)
                    if sc >= 2:
                        e.wait_ge(sp2a[sc % 2], 16 * (sc // 2))
                    for mt in range(8):
                        cp = e.copy(out=qstg[sc % 2][:, mt * 256:(mt + 1) * 256],
                                    in_=ps_q[mt][:])
                        if mt < 7:
                            cp.then_inc(se2a, 1)
                    e.copy(out=dum_sb[:], in_=dum_sb[:]).then_inc(se2a, 1)

            @block.vector
            def _(e):
                def hj(apx):
                    return apx.rearrange("p (h j) -> p h j", h=8, j=128)
                for it in range(32):
                    p = it % 2
                    e.wait_ge(ld_wq[it % 2], 16 * (it // 2 + 1))
                    wv_ = wq_sb[p][:].rearrange("p (h gj) -> p h gj", h=8, gj=512)
                    e.tensor_tensor(hj(ta[p][:]), wv_[:, :, 0:128], wv_[:, :, 128:256], ALU.add)
                    e.tensor_tensor(hj(tb[p][:]), wv_[:, :, 256:384], wv_[:, :, 384:512],
                                    ALU.add).then_inc(ve_ab, 1)
                    if it >= 1:
                        pp = (it - 1) % 2
                        e.tensor_tensor(hj(wq_eff[:, (it - 1) * 1024: it * 1024]),
                                        hj(ta[pp][:]), hj(tb[pp][:]), ALU.add)
                        e.engine_nop()
                        e.engine_nop().then_inc(ve_eff, 1)
                e.engine_nop()
                e.tensor_tensor(hj(wq_eff[:, 31 * 1024: 32 * 1024]),
                                hj(ta[1][:]), hj(tb[1][:]), ALU.add)
                e.engine_nop()
                e.engine_nop().then_inc(ve_eff, 1)

            @block.sync
            def _(e):
                def spill(sc):
                    e.wait_ge(se2a, 8 * (sc + 1))
                    e.dma_start(out=qT_sp[:, :, sc * 256:(sc + 1) * 256],
                                in_=qstg[sc % 2][:]).then_inc(sp2a[sc % 2], 16)
                for sc in range(8):
                    if sc >= 2:
                        e.wait_ge(te2a, sc - 1)
                    e.dma_start(out=xc_sb[sc % 2][:],
                                in_=xnT_sp[:, :, sc * 256:(sc + 1) * 256]).then_inc(ld_xc[sc % 2], 16)
                    if sc >= 2:
                        spill(sc - 2)
                spill(6)
                spill(7)
                e.wait_ge(sp2a[0], 16 * 4)
                e.wait_ge(sp2a[1], 16 * 4)

            @block.tensor
            def _(e):
                for sc in range(8):
                    e.wait_ge(ld_xc[sc % 2], 16 * (sc // 2 + 1))
                    if sc >= 1:
                        e.wait_ge(se2a, 8 * sc)
                    for it in range(32):
                        if sc == 0:
                            e.wait_ge(ve_eff, min(it + 2, 32))
                        for mt in range(8):
                            mm = e.matmul(ps_q[mt][:],
                                          wq_eff[:, it * 1024 + mt * 128: it * 1024 + (mt + 1) * 128],
                                          xc_sb[sc % 2][:, it * 256:(it + 1) * 256],
                                          start=(it == 0), stop=(it == 31))
                    mm.then_inc(te2a, 1)

            barrier()

        def proj_kv(which):
            with ExitStack() as ctx:
                E = ctx.enter_context
                w_sb = E(nc.sbuf_tensor(f"w_sb_{which}", [128, NT * 1024], BF16))
                c_sb = [E(nc.sbuf_tensor(f"c_sb_{which}{i}", [128, NT * 512], BF16)) for i in range(2)]
                kstg = [E(nc.sbuf_tensor(f"stg_{which}{i}", [128, 8 * 512], BF16)) for i in range(2)]
                ps = [E(nc.psum_tensor(f"ps_{which}{i}", [128, 512], F32)) for i in range(2)]

                ld_w = GE(nc.semaphore(f"ld_w{which}"))
                ld_c = [GE(nc.semaphore(f"ld_c{which}{i}")) for i in range(2)]
                te = GE(nc.semaphore(f"te2b{which}"))
                se = GE(nc.semaphore(f"se2b{which}"))
                sp = [GE(nc.semaphore(f"sp2b{which}{i}")) for i in range(2)]
                wsrc = wk if which == "k" else wv
                n_per_sc = 8 if which == "k" else 4

                @block.gpsimd
                def _(e):
                    for it in range(32):
                        e.dma_start(out=w_sb[:, it * 1024:(it + 1) * 1024],
                                    in_=wsrc[it * 128:(it + 1) * 128, :]).then_inc(ld_w, 16)

                @block.sync
                def _(e):
                    def spill_k(sc):
                        e.wait_ge(se, 8 * (sc + 1))
                        e.dma_start(out=kT_sp[:, :, sc * 512:(sc + 1) * 512],
                                    in_=kstg[sc % 2][:]).then_inc(sp[sc % 2], 16)

                    def spill_v(gi):
                        e.wait_ge(se, 2 * (gi + 1))
                        e.dma_start(out=v_sp[:, gi, :],
                                    in_=kstg[gi % 2][:, 0:1024]).then_inc(sp[gi % 2], 16)

                    for sc in range(8):
                        if sc >= 2:
                            if which == "k":
                                spill_k(sc - 2)
                            else:
                                for gi in range((sc - 2) * 4, (sc - 1) * 4):
                                    spill_v(gi)
                            e.wait_ge(te, (sc - 1) * n_per_sc)
                        e.dma_start(out=c_sb[sc % 2][:],
                                    in_=xnT_sp[:, :, sc * 512:(sc + 1) * 512]).then_inc(ld_c[sc % 2], 16)
                    if which == "k":
                        spill_k(6)
                        spill_k(7)
                        e.wait_ge(sp[0], 16 * 4)
                        e.wait_ge(sp[1], 16 * 4)
                    else:
                        for gi in range(24, 32):
                            spill_v(gi)
                        e.wait_ge(sp[0], 16 * 16)
                        e.wait_ge(sp[1], 16 * 16)

                if which == "k":
                    @block.tensor
                    def _(e):
                        e.wait_ge(ld_w, 16 * 32)
                        for sc in range(8):
                            e.wait_ge(ld_c[sc % 2], 16 * (sc // 2 + 1))
                            for mt in range(8):
                                gi = sc * 8 + mt
                                if gi >= 2:
                                    e.wait_ge(se, gi - 1)
                                for it in range(32):
                                    mm = e.matmul(ps[mt % 2][:],
                                                  w_sb[:, it * 1024 + mt * 128: it * 1024 + (mt + 1) * 128],
                                                  c_sb[sc % 2][:, it * 512:(it + 1) * 512],
                                                  start=(it == 0), stop=(it == 31))
                                mm.then_inc(te, 1)

                    @block.scalar
                    def _(e):
                        for sc in range(8):
                            if sc >= 2:
                                e.wait_ge(sp[sc % 2], 16 * (sc // 2))
                            for mt in range(8):
                                gi = sc * 8 + mt
                                e.wait_ge(te, gi + 1)
                                cp = e.copy(out=kstg[sc % 2][:, mt * 512:(mt + 1) * 512],
                                            in_=ps[mt % 2][:])
                                if mt < 7:
                                    cp.then_inc(se, 1)
                            e.copy(out=dum_sb[:], in_=dum_sb[:]).then_inc(se, 1)
                else:
                    @block.tensor
                    def _(e):
                        e.wait_ge(ld_w, 16 * 32)
                        for sc in range(8):
                            e.wait_ge(ld_c[sc % 2], 16 * (sc // 2 + 1))
                            for smt in range(4):
                                gi = sc * 4 + smt
                                if gi >= 1:
                                    e.wait_ge(se, 2 * gi)
                                for nsl in range(2):
                                    for it in range(32):
                                        mm = e.matmul(ps[nsl][:],
                                                      c_sb[sc % 2][:, it * 512 + smt * 128: it * 512 + (smt + 1) * 128],
                                                      w_sb[:, it * 1024 + nsl * 512: it * 1024 + (nsl + 1) * 512],
                                                      start=(it == 0), stop=(it == 31))
                                mm.then_inc(te, 1)

                    @block.scalar
                    def _(e):
                        for sc in range(8):
                            for smt in range(4):
                                gi = sc * 4 + smt
                                e.wait_ge(te, gi + 1)
                                if gi >= 2:
                                    e.wait_ge(sp[gi % 2], 16 * (gi // 2))
                                e.copy(out=kstg[smt % 2][:, 0:512], in_=ps[0][:]).then_inc(se, 1)
                                e.copy(out=kstg[smt % 2][:, 512:1024], in_=ps[1][:])
                                e.copy(out=dum_sb[:], in_=dum_sb[:]).then_inc(se, 1)

                barrier()

        proj_kv("k")
        proj_kv("v")

        with ExitStack() as ctx:
            E = ctx.enter_context
            kt_sb = E(nc.sbuf_tensor("kt_sb", [128, 4 * S], BF16))
            vg_sb = E(nc.sbuf_tensor("vg_sb", [128, NT * 512], BF16))
            qt_sb = E(nc.sbuf_tensor("qt_sb", [128, 4 * OWN], BF16))
            expT = [E(nc.sbuf_tensor(f"expT{i}", [128, 1024], BF16)) for i in range(2)]
            rec_sb = E(nc.sbuf_tensor("rec_sb", [1, 512], F32))
            bc_sb = E(nc.sbuf_tensor("bc_sb", [128, 512], F32))
            ps_s2 = [E(nc.psum_tensor(f"ps_s2{i}", [128, 1024], F32)) for i in range(2)]
            ps_pv = [E(nc.psum_tensor(f"ps_pv{i}", [128, 512], F32)) for i in range(2)]
            ps_dn = [E(nc.psum_tensor(f"ps_dn{i}", [1, 512], F32)) for i in range(2)]

            ld3 = GE(nc.semaphore("ld3"))
            te_s = GE(nc.semaphore("te_s"))
            se_e = GE(nc.semaphore("se_e"))
            te_pvk = GE(nc.semaphore("te_pvk"))
            ve_r = GE(nc.semaphore("ve_r"))
            te_bc = GE(nc.semaphore("te_bc"))
            se_bc = GE(nc.semaphore("se_bc"))
            ve_av = GE(nc.semaphore("ve_av"))

            @block.sync
            def _(e):
                for hg in range(2):
                    if hg == 1:
                        e.wait_ge(te_bc, 16)
                    for hh in range(4):
                        e.dma_start(out=kt_sb[:, hh * S:(hh + 1) * S],
                                    in_=kT_sp[:, hg * 4 + hh, :]).then_inc(ld3, 16)
                        e.dma_start(out=qt_sb[:, hh * OWN:(hh + 1) * OWN],
                                    in_=qT_sp[:, hg * 4 + hh, :]).then_inc(ld3, 16)
                    e.dma_start(out=vg_sb[:], in_=v_sp[:, :, hg * 512:(hg + 1) * 512]).then_inc(ld3, 16)

            @block.tensor
            def _(e):
                for hg in range(2):
                    e.wait_ge(ld3, 16 * 9 * (hg + 1))
                    for hh in range(4):
                        for tc in range(4):
                            m = (hg * 4 + hh) * 4 + tc
                            if m >= 2:
                                e.wait_ge(ve_av, m - 1)
                            if m >= 1:
                                e.wait_ge(se_bc, m)
                            for kp in range(16):
                                g = m * 16 + kp
                                if g >= 2:
                                    e.wait_ge(se_e, g - 1)
                                for j in range(2):
                                    kt = 2 * kp + j
                                    mm = e.matmul(ps_s2[kp % 2][:, j * 512:(j + 1) * 512],
                                                  kt_sb[:, hh * S + kt * 128: hh * S + (kt + 1) * 128],
                                                  qt_sb[:, hh * OWN + tc * 512: hh * OWN + (tc + 1) * 512],
                                                  start=True, stop=True)
                                mm.then_inc(te_s, 1)
                                if kp >= 1:
                                    kq = kp - 1
                                    e.wait_ge(se_e, m * 16 + kq + 1)
                                    for j in range(2):
                                        kt = 2 * kq + j
                                        e.matmul(ps_pv[m % 2][:],
                                                 vg_sb[:, kt * 512 + hh * 128: kt * 512 + (hh + 1) * 128],
                                                 expT[kq % 2][:, j * 512:(j + 1) * 512],
                                                 start=(kt == 0), stop=(kt == 31))
                                        mm = e.matmul(ps_dn[m % 2][:], ones1[:],
                                                      expT[kq % 2][:, j * 512:(j + 1) * 512],
                                                      start=(kt == 0), stop=(kt == 31))
                                    mm.then_inc(te_pvk, 1)
                            kq = 15
                            e.wait_ge(se_e, m * 16 + kq + 1)
                            for j in range(2):
                                kt = 2 * kq + j
                                e.matmul(ps_pv[m % 2][:],
                                         vg_sb[:, kt * 512 + hh * 128: kt * 512 + (hh + 1) * 128],
                                         expT[kq % 2][:, j * 512:(j + 1) * 512],
                                         start=(kt == 0), stop=(kt == 31))
                                mm = e.matmul(ps_dn[m % 2][:], ones1[:],
                                              expT[kq % 2][:, j * 512:(j + 1) * 512],
                                              start=(kt == 0), stop=(kt == 31))
                            mm.then_inc(te_pvk, 1)
                            e.wait_ge(ve_r, m + 1)
                            e.matmul(ps_s2[0][:, 0:512], onesr[:], rec_sb[:],
                                     start=True, stop=True).then_inc(te_bc, 1)

            @block.scalar
            def _(e):
                for m in range(32):
                    for kp in range(16):
                        g = m * 16 + kp
                        e.wait_ge(te_s, g + 1)
                        if g >= 2:
                            e.wait_ge(te_pvk, g - 1)
                        e.activation(expT[kp % 2][:], ps_s2[kp % 2][:], AF.Exp,
                                     scale=SC)
                        e.copy(out=dum_sb[:], in_=dum_sb[:]).then_inc(se_e, 1)
                    e.wait_ge(te_bc, m + 1)
                    if m >= 1:
                        e.wait_ge(ve_av, m)
                    e.copy(out=bc_sb[:], in_=ps_s2[0][:, 0:512])
                    e.copy(out=dum_sb[:], in_=dum_sb[:]).then_inc(se_bc, 1)

            @block.vector
            def _(e):
                for m in range(32):
                    hd = m // 4
                    tc = m % 4
                    e.wait_ge(te_pvk, 16 * (m + 1))
                    if m >= 1:
                        e.wait_ge(te_bc, m)
                    e.reciprocal(rec_sb[:], ps_dn[m % 2][0:1, :])
                    e.engine_nop()
                    e.engine_nop().then_inc(ve_r, 1)
                    e.wait_ge(se_bc, m + 1)
                    e.scalar_tensor_tensor(av_sb[:, hd * OWN + tc * 512: hd * OWN + (tc + 1) * 512],
                                           ps_pv[m % 2][:], 1.0, bc_sb[:],
                                           ALU.mult, ALU.mult).then_inc(ve_av, 1)

            @block.gpsimd
            def _(e):
                for k in range(4):
                    for hf in range(2):
                        e.dma_start(out=bases_bf[k, hf * 2048:(hf + 1) * 2048, :],
                                    in_=bases[k, hf * 2048:(hf + 1) * 2048, :],
                                    max_dma_last_dim=2048).then_inc(ld0, 16)

            barrier()

        with ExitStack() as ctx:
            E = ctx.enter_context
            b_sb = [E(nc.sbuf_tensor(f"b_sb{i}", [128, D], BF16)) for i in range(4)]
            wq_ring = [E(nc.sbuf_tensor(f"wq_ring{i}", [128, D], BF16)) for i in range(8)]
            wstg = [E(nc.sbuf_tensor(f"wstg{i}", [128, 16 * 512], BF16)) for i in range(2)]
            ps_t2 = [E(nc.psum_tensor(f"ps_t2{i}", [128, 512], BF16)) for i in range(2)]

            ld_b = [GE(nc.semaphore(f"ld_b{i}")) for i in range(4)]
            ve_b = GE(nc.semaphore("ve_b"))
            ve_acc = GE(nc.semaphore("ve_acc"))
            te2c = GE(nc.semaphore("te2c"))
            ve_ev2 = GE(nc.semaphore("ve_ev2"))
            sp2c = [GE(nc.semaphore(f"sp2c{i}")) for i in range(2)]

            @block.sync
            def _(e):
                e.wait_ge(ld0, 16 * 41)
                for op in range(32):
                    for k in range(4):
                        gi = op * 4 + k
                        if gi >= 4:
                            e.wait_ge(ve_b, gi - 3)
                        e.dma_start(out=b_sb[gi % 4][:],
                                    in_=bases_bf[k, op * 128:(op + 1) * 128, :]).then_inc(ld_b[gi % 4], 16)

            @block.vector
            def _(e):
                def emit_build(op):
                    r = wq_ring[op % 8]
                    og = op // 4
                    if og >= 2:
                        e.wait_ge(te2c, (og - 1) * 32)
                    for k in range(4):
                        gi = op * 4 + k
                        e.wait_ge(ld_b[gi % 4], 16 * (gi // 4 + 1))
                        if k == 0:
                            e.tensor_scalar(r[:], b_sb[gi % 4][:],
                                            al_sb[:, op * 4: op * 4 + 1], None,
                                            ALU.mult).then_inc(ve_b, 1)
                        else:
                            e.scalar_tensor_tensor(r[:], b_sb[gi % 4][:],
                                                   al_sb[:, op * 4 + k: op * 4 + k + 1], r[:],
                                                   ALU.mult, ALU.add).then_inc(ve_b, 1)
                        e.engine_nop()
                    e.engine_nop().then_inc(ve_acc, 1)

                def emit_evac_unit(u):
                    og, hf = u // 2, u % 2
                    if u >= 2:
                        e.wait_ge(sp2c[u % 2], 16 * (u // 2))
                    for itl in range(16):
                        it = hf * 16 + itl
                        gi = og * 32 + it
                        e.wait_ge(te2c, gi + 1)
                        cp = e.tensor_copy(wstg[u % 2][:, itl * 512:(itl + 1) * 512],
                                           ps_t2[it % 2][:])
                        if itl < 15:
                            cp.then_inc(ve_ev2, 1)
                    e.engine_nop()
                    e.engine_nop().then_inc(ve_ev2, 1)

                e.wait_ge(ld0, 16 * 33)
                for op in range(8):
                    emit_build(op)
                for og in range(2, 8):
                    emit_evac_unit((og - 2) * 2)
                    emit_evac_unit((og - 2) * 2 + 1)
                    for j in range(4):
                        emit_build(og * 4 + j)
                for u in range(12, 16):
                    emit_evac_unit(u)

            @block.tensor
            def _(e):
                for og in range(8):
                    e.wait_ge(ve_acc, 4 * (og + 1))
                    for it in range(32):
                        gi = og * 32 + it
                        if gi >= 2:
                            e.wait_ge(ve_ev2, gi - 1)
                        for j in range(4):
                            mm = e.transpose(ps_t2[it % 2][:, j * 128:(j + 1) * 128],
                                             wq_ring[(og * 4 + j) % 8][:, it * 128:(it + 1) * 128],
                                             ident_sb[:])
                        mm.then_inc(te2c, 1)

            @block.scalar
            def _(e):
                for u in range(16):
                    og, hf = u // 2, u % 2
                    e.wait_ge(ve_ev2, 16 * (u + 1))
                    e.dma_start(out=wqT_sp[og, :, hf * 16:(hf + 1) * 16, :],
                                in_=wstg[u % 2][:]).then_inc(sp2c[u % 2], 16)
                e.wait_ge(sp2c[0], 16 * 8)
                e.wait_ge(sp2c[1], 16 * 8)

            barrier()

        with ExitStack() as ctx:
            E = ctx.enter_context
            xh_sb = E(nc.sbuf_tensor("xh_sb", [128, NT * 1024], BF16))
            wqT_sb = [E(nc.sbuf_tensor(f"wqT_sb{i}", [128, NT * 512], BF16)) for i in range(2)]
            g_st = [E(nc.sbuf_tensor(f"g_st{i}", [128, 512], BF16)) for i in range(2)]
            ps_f = [E(nc.psum_tensor(f"ps_f{i}", [128, 512], F32)) for i in range(2)]

            ld4x = GE(nc.semaphore("ld4x"))
            ld4w = [GE(nc.semaphore(f"ld4w{i}")) for i in range(2)]
            te4 = GE(nc.semaphore("te4"))
            se4 = GE(nc.semaphore("se4"))
            sp4 = [GE(nc.semaphore(f"sp4_{i}")) for i in range(2)]

            @block.sync
            def _(e):
                def spill_g(gj):
                    rh2, oc2, mt2 = gj // 64, (gj // 8) % 8, gj % 8
                    e.wait_ge(se4, gj + 1)
                    e.dma_start(out=g_sp[rh2 * 1024 + mt2 * 128: rh2 * 1024 + (mt2 + 1) * 128,
                                         oc2 * 512:(oc2 + 1) * 512],
                                in_=g_st[gj % 2][:]).then_inc(sp4[gj % 2], 16)
                e.dma_start(out=xh_sb[:], in_=xnT_sp[:, :, 0:1024]).then_inc(ld4x, 16)
                for gj in range(64):
                    spill_g(gj)
                e.wait_ge(te4, 64)
                e.dma_start(out=xh_sb[:], in_=xnT_sp[:, :, 1024:2048]).then_inc(ld4x, 16)
                for gj in range(64, 128):
                    spill_g(gj)
                e.wait_ge(sp4[0], 16 * 64)
                e.wait_ge(sp4[1], 16 * 64)

            @block.gpsimd
            def _(e):
                for rh in range(2):
                    for oc in range(8):
                        gi = rh * 8 + oc
                        if gi >= 2:
                            e.wait_ge(te4, (gi - 1) * 8)
                        e.dma_start(out=wqT_sb[gi % 2][:], in_=wqT_sp[oc, :, :, :]).then_inc(ld4w[gi % 2], 16)

            @block.tensor
            def _(e):
                for rh in range(2):
                    e.wait_ge(ld4x, 16 * (rh + 1))
                    for oc in range(8):
                        gi = rh * 8 + oc
                        e.wait_ge(ld4w[gi % 2], 16 * (gi // 2 + 1))
                        for mt in range(8):
                            gj = gi * 8 + mt
                            if gj >= 2:
                                e.wait_ge(se4, gj - 1)
                            for it in range(32):
                                mm = e.matmul(ps_f[mt % 2][:],
                                              xh_sb[:, it * 1024 + mt * 128: it * 1024 + (mt + 1) * 128],
                                              wqT_sb[gi % 2][:, it * 512:(it + 1) * 512],
                                              start=(it == 0), stop=(it == 31))
                            mm.then_inc(te4, 1)

            @block.scalar
            def _(e):
                for rh in range(2):
                    for oc in range(8):
                        for mt in range(8):
                            gj = (rh * 8 + oc) * 8 + mt
                            e.wait_ge(te4, gj + 1)
                            if gj >= 2:
                                e.wait_ge(sp4[gj % 2], 16 * (gj // 2))
                            e.activation(g_st[gj % 2][:], ps_f[mt % 2][:], AF.Silu)
                            e.copy(out=dum_sb[:], in_=dum_sb[:]).then_inc(se4, 1)

            barrier()

        with ExitStack() as ctx:
            E = ctx.enter_context
            wo_sb = E(nc.sbuf_tensor("wo_sb", [128, 8 * D], BF16))
            g_in = [E(nc.sbuf_tensor(f"g_in{i}", [128, 512], BF16)) for i in range(2)]
            x_in = [E(nc.sbuf_tensor(f"x_in{i}", [128, 512], F32)) for i in range(2)]
            t_sb = [E(nc.sbuf_tensor(f"t_sb{i}", [128, 512], F32)) for i in range(2)]
            o_st = [E(nc.sbuf_tensor(f"o_st{i}", [128, 512], F32)) for i in range(2)]
            ps_o = [E(nc.psum_tensor(f"ps_o{i}", [128, 512], F32)) for i in range(2)]

            ld5w = GE(nc.semaphore("ld5w"))
            ldg = [GE(nc.semaphore(f"ldg{i}")) for i in range(2)]
            ldx = [GE(nc.semaphore(f"ldx{i}")) for i in range(2)]
            te5 = GE(nc.semaphore("te5"))
            ve5s = GE(nc.semaphore("ve5s"))
            ve5t = GE(nc.semaphore("ve5t"))
            sp5 = [GE(nc.semaphore(f"sp5_{i}")) for i in range(2)]

            @block.gpsimd
            def _(e):
                e.wait_ge(ld0, 16 * 41)
                for kt in range(8):
                    e.dma_start(out=wo_sb[:, kt * D:(kt + 1) * D],
                                in_=wo_bf[kt * 128:(kt + 1) * 128, :]).then_inc(ld5w, 16)

            @block.sync
            def _(e):
                for i in range(128):
                    mt, dc = i // 8, i % 8
                    if i >= 2:
                        e.wait_ge(ve5s, i - 1)
                        e.wait_ge(ve5t, i - 1)
                    e.dma_start(out=g_in[i % 2][:],
                                in_=g_sp[mt * 128:(mt + 1) * 128, dc * 512:(dc + 1) * 512]).then_inc(ldg[i % 2], 16)
                    e.dma_start(out=x_in[i % 2][:],
                                in_=x[mt * 128:(mt + 1) * 128, dc * 512:(dc + 1) * 512]).then_inc(ldx[i % 2], 16)

            @block.tensor
            def _(e):
                e.wait_ge(ld5w, 16 * 8)
                for i in range(128):
                    mt, dc = i // 8, i % 8
                    if i >= 2:
                        e.wait_ge(ve5s, i - 1)
                    for kt in range(8):
                        mm = e.matmul(ps_o[i % 2][:],
                                      av_sb[:, kt * OWN + mt * 128: kt * OWN + (mt + 1) * 128],
                                      wo_sb[:, kt * D + dc * 512: kt * D + (dc + 1) * 512],
                                      start=(kt == 0), stop=(kt == 7))
                    mm.then_inc(te5, 1)

            @block.vector
            def _(e):
                for i in range(128):
                    e.wait_ge(te5, i + 1)
                    e.wait_ge(ldg[i % 2], 16 * (i // 2 + 1))
                    e.scalar_tensor_tensor(t_sb[i % 2][:], ps_o[i % 2][:], 1.0, g_in[i % 2][:],
                                           ALU.mult, ALU.mult).then_inc(ve5s, 1)
                    if i >= 1:
                        j = i - 1
                        e.wait_ge(ldx[j % 2], 16 * (j // 2 + 1))
                        if j >= 2:
                            e.wait_ge(sp5[j % 2], 16 * (j // 2))
                        e.tensor_tensor(o_st[j % 2][:], t_sb[j % 2][:], x_in[j % 2][:],
                                        ALU.add)
                        e.engine_nop().then_inc(ve5t, 1)
                j = 127
                e.engine_nop()
                e.wait_ge(ldx[j % 2], 16 * (j // 2 + 1))
                e.wait_ge(sp5[j % 2], 16 * (j // 2))
                e.tensor_tensor(o_st[j % 2][:], t_sb[j % 2][:], x_in[j % 2][:],
                                ALU.add)
                e.engine_nop().then_inc(ve5t, 1)

            @block.scalar
            def _(e):
                for i in range(128):
                    mt, dc = i // 8, i % 8
                    e.wait_ge(ve5t, i + 1)
                    e.dma_start(out=out[mt * 128:(mt + 1) * 128, dc * 512:(dc + 1) * 512],
                                in_=o_st[i % 2][:]).then_inc(sp5[i % 2], 16)
                e.wait_ge(sp5[0], 16 * 64)
                e.wait_ge(sp5[1], 16 * 64)

    return nc


_NC_CACHE = None


def kernel(L0_attn_x, L0_attn_wq, L0_attn_wk, L0_attn_wv, L0_attn_wo, L0_ffn_bases, L0_ffn_alphas):
    global _NC_CACHE
    x = np.ascontiguousarray(np.asarray(L0_attn_x, dtype=np.float32))
    wq = np.ascontiguousarray(np.asarray(L0_attn_wq, dtype=np.float32))
    wk = np.ascontiguousarray(np.asarray(L0_attn_wk, dtype=np.float32))
    wv = np.ascontiguousarray(np.asarray(L0_attn_wv, dtype=np.float32))
    wo = np.ascontiguousarray(np.asarray(L0_attn_wo, dtype=np.float32))
    bases = np.ascontiguousarray(np.asarray(L0_ffn_bases, dtype=np.int8))
    alphas_t = np.ascontiguousarray(np.asarray(L0_ffn_alphas, dtype=np.float32).reshape(4, D).T)
    ident = np.eye(128, dtype=ml_dtypes.bfloat16)

    if _NC_CACHE is None:
        _NC_CACHE = build_nc()
    nc = _NC_CACHE

    in_maps = []
    for c in range(8):
        b, h = c // 2, c % 2
        xb = x[b]
        xp = np.ascontiguousarray(
            np.concatenate([xb[h * OWN:(h + 1) * OWN], xb[(1 - h) * OWN:(2 - h) * OWN]], axis=0))
        in_maps.append({"x": xp, "wq": wq, "wk": wk, "wv": wv, "wo": wo,
                        "bases": bases, "alphas_t": alphas_t, "ident": ident})

    res = run_bass_kernel_spmd(nc, in_maps, core_ids=list(range(8)))

    outa = np.empty((4, S, D), np.float32)
    for c in range(8):
        b, h = c // 2, c % 2
        outa[b, h * OWN:(h + 1) * OWN, :] = res.results[c]["out"]
    return outa
